# version 25
# baseline (speedup 1.0000x reference)
import numpy as np
import ml_dtypes
from contextlib import ExitStack

import concourse.bass as bass
import concourse.mybir as mybir
from concourse.bass_utils import run_bass_kernel_spmd

F32 = mybir.dt.float32
BF16 = mybir.dt.bfloat16
ALU = mybir.AluOpType
ACTF = mybir.ActivationFunctionType
AX = mybir.AxisListType.X

MAGIC = 12582912.0
T, D, H = 4096, 1024, 8
F1, FQ = 4096, 3072
EPS = 1e-5
NEWTON_ITERS = 3
RACE_DETECT = False
DEBUG_DUMP = False

ENGS = ("sync", "vector", "scalar", "tensor", "gpsimd")


class Sched:
    def __init__(self, nc, ctx):
        self.nc = nc
        self.ctx = ctx
        self.stream = {e: [] for e in ENGS}
        self.phase = -1
        self.sems = {}
        self.count = {}
        self.lastwait = {e: {} for e in ENGS}
        self.resw = {}
        self.resr = {}
        self.new_phase()

    DMA_POOL = {"sync": 16, "gpsimd": 10, "scalar": 8, "vector": 2, "tensor": 2}

    def new_phase(self):
        self.phase += 1
        self.dmapool = {}
        self.dmaidx = {}
        for e in ENGS:
            name = f"sem_{e}_{self.phase}"
            sem = self.ctx.enter_context(self.nc.semaphore(name))
            self.sems[e] = (name, sem)
            self.count[name] = 0
            pool = []
            for i in range(self.DMA_POOL[e]):
                dn = f"dma_{e}_{self.phase}_{i}"
                ds = self.ctx.enter_context(self.nc.semaphore(dn))
                pool.append((dn, ds))
                self.count[dn] = 0
            self.dmapool[e] = pool
            self.dmaidx[e] = 0

    def emit(self, eng, build, reads=(), writes=(), dma=False):
        if dma:
            pool = self.dmapool[eng]
            i = self.dmaidx[eng] % len(pool)
            self.dmaidx[eng] += 1
            name, sem = pool[i]
        else:
            name, sem = self.sems[eng]
        waits = {}

        def addwait(h):
            if h is None:
                return
            heng, hname, hsem, hval, hdma, hcarry = h
            if heng == "tensor" and eng == "tensor" and not hdma and not dma:
                return
            cur = waits.get(hname)
            if cur is None or hval > cur[1]:
                waits[hname] = (hsem, hval)
            for cname, (csem, cval) in hcarry:
                cur = waits.get(cname)
                if cur is None or cval > cur[1]:
                    waits[cname] = (csem, cval)

        for r in reads:
            addwait(self.resw.get(r))
        for r in writes:
            addwait(self.resw.get(r))
            for h in self.resr.get(r, ()):
                addwait(h)
        if dma and self.count[name] > 0:
            waits[name] = (sem, self.count[name])

        lw = self.lastwait[eng]
        fw = []
        for hname, (hsem, hval) in waits.items():
            if not dma and lw.get(hname, -1) >= hval:
                continue
            if lw.get(hname, -1) < hval:
                lw[hname] = hval
            fw.append((hsem, hval))

        inc = 16 if dma else 1
        self.count[name] += inc
        carry = ()
        myh = (eng, name, sem, self.count[name], dma, carry)

        def closure(e, _w=tuple(fw), _b=build, _s=sem, _i=inc):
            for ws, wv in _w:
                e.wait_ge(ws, wv)
            _b(e).then_inc(_s, _i)

        self.stream[eng].append(closure)
        for r in reads:
            self.resr.setdefault(r, []).append(myh)
        for r in writes:
            self.resw[r] = myh
            self.resr[r] = []
        return myh

    def barrier_all(self):
        targets = []
        for e2 in ENGS:
            nm, sem = self.sems[e2]
            targets.append((nm, sem, self.count[nm]))
            for nm2, sem2 in self.dmapool[e2]:
                targets.append((nm2, sem2, self.count[nm2]))
        for e in ENGS:
            lw = self.lastwait[e]
            for nm, sem, val in targets:
                if val > 0 and lw.get(nm, -1) < val:
                    lw[nm] = val
                    self.stream[e].append(lambda en, s=sem, v=val: en.wait_ge(s, v))

    def finalize(self, block):
        for ename in ENGS:
            ops = self.stream[ename]
            getattr(block, ename)(lambda e, _ops=ops: [f(e) for f in _ops])


def _consts_np():
    i = np.arange(128)
    ident = np.eye(128, dtype=np.float32)
    return {
        "c_idf": ident,
        "c_n2i": (-2.0 * ident).astype(ml_dtypes.bfloat16),
        "c_2i": (2.0 * ident).astype(ml_dtypes.bfloat16),
        "c_idb": ident.astype(ml_dtypes.bfloat16),
        "c_tri": (i[:, None] <= i[None, :]).astype(np.float32),
        "c_sgt": (i[:, None] > i[None, :]).astype(np.float32),
        "c_lmts": np.where(i[None, :] < i[:, None], 0.0, -3e38).astype(np.float32),
        "c_lmst": np.where(i[:, None] < i[None, :], 0.0, -3e38).astype(np.float32),
    }


def build_program(T=T):
    NCH = T // 128
    NTT = T // 512
    BW = min(T, 4096)
    nc = bass.Bass("TRN2", target_bir_lowering=False, debug=False,
                   detect_race_conditions=RACE_DETECT)
    d = {}

    def din(name, shape, dt=F32):
        d[name] = nc.dram_tensor(name, list(shape), dt, kind="ExternalInput").ap()

    def dint(name, shape, dt=F32):
        d[name] = nc.dram_tensor(name, list(shape), dt).ap()

    din("x0", (T, D))
    din("Wq", (D, D)); din("Wk", (D, D)); din("Wv", (D, D))
    din("Wa", (D, H)); din("Wb", (D, H)); din("Wo", (D, D))
    din("ln1w", (D,)); din("ln1b", (D,)); din("ln2w", (D,)); din("ln2b", (D,))
    din("W1", (D, F1)); din("b1", (F1,)); din("W2", (F1, D)); din("b2", (D,))
    din("wup", (FQ, D)); din("wgt", (FQ, D)); din("wdn", (D, FQ))
    for nme in ("c_idf", "c_tri", "c_sgt", "c_lmts", "c_lmst"):
        din(nme, (128, 128), F32)
    for nme in ("c_idb", "c_n2i", "c_2i"):
        din(nme, (128, 128), BF16)
    d["out"] = nc.dram_tensor("out", [T, D], F32, kind="ExternalOutput").ap()
    dint("xa_h", (T, D)); dint("x1_h", (T, D))
    dint("w1e_h", (8, 128, 8, 512), BF16)
    dint("w2b_h", (32, 128, 1024), BF16)
    dint("upT_h", (24, 128, 1024), BF16)
    dint("gtT_h", (24, 128, 1024), BF16)
    dint("dnT_h", (24, 128, 1024), BF16)
    dint("rup_h", (24, 128, T)); dint("rgt_h", (24, 128, T))
    if DEBUG_DUMP:
        dint("dbgU_h", (T // 128, 128, 1024)); dint("dbgO_h", (T // 128, 128, 1024))
        dint("dbgM_h", (T // 128, 128, 1024)); dint("dbgG_h", (T // 128, 128, 1024))
    dint("tt_h", (24, 128, T)); dint("ss_h", (24, 128, T))
    for k in range(4):
        dint(f"ccb{k}", (1, 2)); dint(f"ccr{k}", (1, 2))

    with ExitStack() as ctx:
        def sb(name, shape, dt):
            return ctx.enter_context(nc.sbuf_tensor(name, list(shape), dt))

        idf = sb("idf", (128, 128), F32)
        idb = sb("idb", (128, 128), BF16)
        n2i = sb("n2i", (128, 128), BF16)
        twoi = sb("twoi", (128, 128), BF16)
        tri = sb("tri", (128, 128), F32)
        sgt = sb("sgt", (128, 128), F32)
        lmts = sb("lmts", (128, 128), F32)
        lmst = sb("lmst", (128, 128), F32)
        onesf = sb("onesf", (1, 128), F32)
        oneb = sb("oneb", (1, 512), BF16)
        cst = sb("cst", (128, 6), F32)
        nc.const_aps.register(F32, MAGIC, cst[:, 0:1])
        nc.const_aps.register(F32, -1.0, cst[:, 1:2])
        nc.const_aps.register(F32, -MAGIC, cst[:, 2:3])
        nc.const_aps.register(F32, 65536.0, cst[:, 3:4])
        nc.const_aps.register(F32, EPS, cst[:, 4:5])
        nc.const_aps.register(F32, 1e-6, cst[:, 5:6])

        wqb = sb("wqb", (128, 8192), BF16)
        wkb = sb("wkb", (128, 8192), BF16)
        wvb = sb("wvb", (128, 8192), BF16)
        wob = sb("wob", (128, 8192), BF16)
        wabb = sb("wabb", (128, 128), BF16)
        brq = sb("brq", (1, 1024), BF16); brk = sb("brk", (1, 1024), BF16)
        brv = sb("brv", (1, 1024), BF16); brab = sb("brab", (1, 16), BF16)
        br1 = sb("br1", (1, 4096), BF16); br2 = sb("br2", (1, 1024), BF16)
        b1s = sb("b1s", (1, 512), F32)
        lnc = sb("lnc", (128, 32), F32)
        wcast = sb("wcast", (128, 1024), BF16)
        x0t = [sb("x0t0", (128, 1024), F32), sb("x0t1", (128, 1024), F32)]
        wld = x0t
        xpt = sb("xpt", (128, 1024), F32)
        scr = sb("scr", (128, 1024), F32)
        qf = sb("qf", (128, 1024), F32)
        kf = sb("kf", (128, 1024), F32)
        vf = sb("vf", (128, 1024), F32)
        Sst = sb("Sst", (128, 1024), F32)
        sm = sb("sm", (128, 448), F32)
        lT1 = sb("lT1", (1, 1024), F32)
        negof = sb("negof", (1, 128), F32)
        rb = sb("rb", (128, 24576), BF16)
        ystat = sb("ystat", (128, 96), F32)
        stg = [sb("stg0", (128, 4096), F32), sb("stg1", (128, 4096), F32)]
        p = [ctx.enter_context(nc.psum_tensor(f"p{i}", [128, 1024], F32))
             for i in range(4)]
        pA, pB, pC, pD = p
        pCb = pC[:].bitcast(BF16)
        PN = ("pA", "pB", "pC")

        block = ctx.enter_context(nc.Block())
        sch = Sched(nc, ctx)

        RB = {}
        for idx, nmm in enumerate(
                ["nb", "nT", "ksc", "qT", "kT", "kpT", "Gs", "GsT", "GiT",
                 "Gp", "Xp0", "S0b", "Ub", "oTb",
                 "G2T0", "G2T1", "Mbv0", "Mbv1", "BT0", "BT1",
                 "qgT0", "qgT1", "ktl0", "ktl1"]):
            RB[nmm] = idx * 1024
        RB["bvb"] = RB["nb"]
        RB["kgb"] = RB["ksc"]
        RB["qsc"] = RB["nb"]
        RB["X1"] = RB["qT"]
        RB["X0"] = RB["kT"]
        RB["Xp1"] = RB["kpT"]
        RB["T1b"] = RB["GiT"]
        RB["T1p"] = RB["GsT"]
        RB["Gn"] = RB["Gs"]
        A2_HT = 1024
        A2_GT = 5120

        def rcols(lo, hi):
            return [f"rb{k}" for k in range(lo // 1024, (hi + 1023) // 1024)]

        def rap(nmm, lo=0, hi=1024):
            base = RB[nmm]
            return rb[:, base + lo:base + hi]

        def rres(nmm, lo=0, hi=1024):
            base = RB[nmm]
            return rcols(base + lo, base + hi)

        SM = {}
        _off = [0]

        SM_OVERLAY = {"ymx2": ("bmx", 48), "ymx3": ("bs", 48)}

        def smcol(nmm, w=1):
            if nmm in SM_OVERLAY:
                base, ww = SM_OVERLAY[nmm]
                assert w == ww
                o, bw = SM[base]
                return sm[:, o:o + w]
            if nmm not in SM:
                SM[nmm] = (_off[0], w)
                _off[0] += w
                assert _off[0] <= 448, nmm
            o, ww = SM[nmm]
            assert ww == w
            return sm[:, o:o + w]

        for _pre in (("bmx", 96), ("bs", 96), ("bi", 96)):
            smcol(*_pre)

        def E(eng, fn, reads=(), writes=(), dma=False):
            return sch.emit(eng, fn, reads=reads, writes=writes, dma=dma)

        def hsl(buf, h):
            return buf[:, h * 128:(h + 1) * 128]

        def psl(pp, h):
            return pp[:, h * 128:(h + 1) * 128]

        def pDres(lo, hi):
            out = []
            if lo < 512:
                out.append("pD_b0")
            if hi > 512:
                out.append("pD_b1")
            return out

        for nmm, t_ in (("c_idf", idf), ("c_idb", idb), ("c_n2i", n2i), ("c_2i", twoi),
                        ("c_tri", tri), ("c_sgt", sgt), ("c_lmts", lmts),
                        ("c_lmst", lmst)):
            E("gpsimd", lambda e, n_=nmm, tt_=t_: e.dma_start(out=tt_[:], in_=d[n_]),
              writes=[nmm], dma=True)
        E("vector", lambda e: e.memset(onesf[:], 1.0), writes=["onesf"])
        E("vector", lambda e: e.memset(oneb[:], 1.0), writes=["oneb"])
        E("vector", lambda e: e.memset(negof[:], -1.0), writes=["negof"])
        E("vector", lambda e: e.memset(cst[:, 0:1], MAGIC), writes=["cst"])
        E("vector", lambda e: e.memset(cst[:, 1:2], -1.0), writes=["cst"])
        E("vector", lambda e: e.memset(cst[:, 2:3], -MAGIC), writes=["cst"])
        E("vector", lambda e: e.memset(cst[:, 3:4], 65536.0), writes=["cst"])
        E("vector", lambda e: e.memset(cst[:, 4:5], EPS), writes=["cst"])
        E("vector", lambda e: e.memset(cst[:, 5:6], 1e-6), writes=["cst"])
        E("vector", lambda e: e.memset(Sst[:], 0.0), writes=["Sst"])
        E("vector", lambda e: e.memset(ystat[:, 64:65], 0.0), writes=["ystat_ymx"])
        E("vector", lambda e: e.memset(ystat[:, 68:69], 0.0), writes=["ystat_um"])
        E("vector", lambda e: e.memset(ystat[:, 69:70], 0.0), writes=["ystat_gm"])
        E("vector", lambda e: e.memset(smcol("ccv", 2), 0.0), writes=["sm_ccv"])
        for j in range(8):
            for nmm, c0 in (("ln1w", 0), ("ln1b", 8), ("ln2w", 16), ("ln2b", 24)):
                E("gpsimd", lambda e, n_=nmm, jj=j, c=c0: e.dma_start(
                    out=lnc[:, c + jj:c + jj + 1],
                    in_=d[n_][jj * 128:(jj + 1) * 128].rearrange("(x o) -> x o", o=1)),
                    writes=["lnc"], dma=True)
        E("gpsimd", lambda e: e.dma_start(out=br2[:], in_=d["b2"].rearrange("(o x) -> o x", o=1)),
          writes=["br2"], dma=True)
        sch.barrier_all()

        def fold_weight(wname, dst, dstres, brow, browres, ncols):
            for j in range(8):
                buf, bres = wld[j % 2], f"x0t{j % 2}"
                E("gpsimd", lambda e, jj=j, b=buf, w=wname: e.dma_start(
                    out=b[:, 0:ncols], in_=d[w][jj * 128:(jj + 1) * 128, :]),
                    writes=[bres], dma=True)
                for o in range(0, ncols, 512):
                    oe = min(o + 512, ncols)
                    E("tensor", lambda e, jj=j, b=buf, o0=o, o1=oe: e.matmul(
                        pD[0:1, o0:o1], lnc[:, 8 + jj:9 + jj], b[:, o0:o1],
                        start=(jj == 0), stop=(jj == 7)),
                        reads=[bres, "lnc"], writes=pDres(o, oe))
                E("vector", lambda e, jj=j, b=buf, dd=dst: e.tensor_scalar(
                    dd[:, jj * ncols:(jj + 1) * ncols], b[:, 0:ncols],
                    lnc[:, jj:jj + 1], None, ALU.mult),
                    reads=[bres, "lnc"], writes=[dstres])
            E("scalar", lambda e: e.copy(brow[0:1, 0:ncols], pD[0:1, 0:ncols]),
              reads=pDres(0, ncols), writes=[browres])

        fold_weight("Wq", wqb, "wqb", brq, "brq", 1024)
        fold_weight("Wk", wkb, "wkb", brk, "brk", 1024)
        fold_weight("Wv", wvb, "wvb", brv, "brv", 1024)
        for wname, coff, boff in (("Wa", 0, 0), ("Wb", 64, 8)):
            for j in range(8):
                buf, bres = wld[j % 2], f"x0t{j % 2}"
                E("gpsimd", lambda e, jj=j, b=buf, w=wname: e.dma_start(
                    out=b[:, 0:8], in_=d[w][jj * 128:(jj + 1) * 128, :]),
                    writes=[bres], dma=True)
                E("tensor", lambda e, jj=j, b=buf, bo=boff: e.matmul(
                    pD[0:1, bo:bo + 8], lnc[:, 8 + jj:9 + jj], b[:, 0:8],
                    start=(jj == 0), stop=(jj == 7)),
                    reads=[bres, "lnc"], writes=["pD_b0"])
                E("vector", lambda e, jj=j, b=buf, co=coff: e.tensor_scalar(
                    wabb[:, co + jj * 8:co + (jj + 1) * 8], b[:, 0:8],
                    lnc[:, jj:jj + 1], None, ALU.mult),
                    reads=[bres, "lnc"], writes=["wabb"])
            E("scalar", lambda e, bo=boff: e.copy(
                brab[0:1, bo:bo + 8], pD[0:1, bo:bo + 8]),
                reads=["pD_b0"], writes=["brab"])
        for j in range(8):
            E("gpsimd", lambda e, jj=j: e.dma_start(
                out=wob[:, jj * 1024:(jj + 1) * 1024],
                in_=d["Wo"][jj * 128:(jj + 1) * 128, :]),
                writes=["wob"], dma=True)
        for j in range(32):
            E("gpsimd", lambda e, jj=j: e.dma_start(
                out=d["w2b_h"][jj], in_=d["W2"][jj * 128:(jj + 1) * 128, :]),
                writes=["w2b_h"], dma=True)

        def stats_ln(src, srcres, tag):
            mc, rc = smcol("m" + tag), smcol("r" + tag)
            E("vector", lambda e: e.reduce_sum(smcol("su" + tag), src, AX),
              reads=[srcres], writes=["sm_su" + tag])
            E("scalar", lambda e: e.activation(scr[:], src, ACTF.Square,
                                               accum_out=smcol("sq" + tag)),
              reads=[srcres], writes=["scr", "sm_sq" + tag])
            E("vector", lambda e: e.tensor_scalar(mc, smcol("su" + tag),
                                                  1.0 / D, None, ALU.mult),
              reads=["sm_su" + tag], writes=["sm_m" + tag])
            E("vector", lambda e: e.tensor_scalar(
                smcol("v" + tag), mc, mc, -1.0, ALU.mult, ALU.mult),
                reads=["sm_m" + tag], writes=["sm_v" + tag])
            E("vector", lambda e: e.scalar_tensor_tensor(
                smcol("v" + tag), smcol("sq" + tag), 1.0 / D, smcol("v" + tag),
                ALU.mult, ALU.add),
                reads=["sm_sq" + tag, "sm_v" + tag], writes=["sm_v" + tag])
            E("scalar", lambda e: e.activation(smcol("v" + tag), smcol("v" + tag),
                                               ACTF.Sqrt, bias=EPS),
              reads=["sm_v" + tag, "cst"], writes=["sm_v" + tag])
            E("vector", lambda e: e.reciprocal(rc, smcol("v" + tag)),
              reads=["sm_v" + tag], writes=["sm_r" + tag])
            return mc, rc, "sm_m" + tag, "sm_r" + tag

        xpt_bufs = [(xpt, ["xpt"]), (stg[0][:, 0:1024], ["stg0_0", "stg0_1"])]

        def stage1(c):
            par = c % 2
            xb, xbres = x0t[par], f"x0t{par}"
            xp_ap, xp_res = xpt_bufs[par]
            E("gpsimd", lambda e, cc=c, b=xb: e.dma_start(
                out=b[:], in_=d["x0"][cc * 128:(cc + 1) * 128, :]),
                writes=[xbres], dma=True)
            m0, r0, m0r, r0r = stats_ln(xb[:], xbres, "0")
            E("vector", lambda e, b=xb, xp=xp_ap: e.tensor_scalar(
                xp[:], b[:], m0, r0, ALU.subtract, ALU.mult),
                reads=[xbres, m0r, r0r] + xp_res, writes=xp_res)
            m1, r1, m1r, r1r = stats_ln(xp_ap[:], xp_res[0] if False else "xp1_" + str(par), "1") \
                if False else stats_ln(xp_ap[:], xp_res[0], "1")
            E("vector", lambda e, xp=xp_ap: e.tensor_scalar(
                rap("nb"), xp[:], m1, r1, ALU.subtract, ALU.mult),
                reads=xp_res + [m1r, r1r], writes=rres("nb"))
            for j in range(8):
                E("tensor", lambda e, jj=j: e.transpose(
                    pCb[:, jj * 128:(jj + 1) * 128],
                    rap("nb", jj * 128, (jj + 1) * 128), idb[:]),
                    reads=rres("nb") + ["c_idb"], writes=["pC"])
            E("scalar", lambda e: e.copy(rap("nT"), pCb[:, 0:1024]),
              reads=["pC"] + rres("nT"), writes=rres("nT"))
            for pi, (wsb, wres, brow, browres) in enumerate(
                    ((wqb, "wqb", brq, "brq"), (wkb, "wkb", brk, "brk"),
                     (wvb, "wvb", brv, "brv"))):
                pp, pres = p[pi], PN[pi]
                for o in range(0, 1024, 512):
                    for j in range(8):
                        E("tensor", lambda e, jj=j, o0=o, _p=pp, _w=wsb: e.matmul(
                            _p[:, o0:o0 + 512], rap("nT", jj * 128, (jj + 1) * 128),
                            _w[:, jj * 1024 + o0:jj * 1024 + o0 + 512],
                            start=(jj == 0), stop=False),
                            reads=rres("nT") + [wres], writes=[pres])
                    E("tensor", lambda e, o0=o, _p=pp, _b=brow: e.matmul(
                        _p[:, o0:o0 + 512], oneb[0:1, 0:128], _b[0:1, o0:o0 + 512],
                        start=False, stop=True),
                        reads=["oneb", browres], writes=[pres])
            E("scalar", lambda e: e.activation(qf[:], pA[:], ACTF.Silu),
              reads=["pA"], writes=["qf"])
            E("scalar", lambda e: e.activation(kf[:], pB[:], ACTF.Silu),
              reads=["pB"], writes=["kf"])
            E("scalar", lambda e: e.activation(vf[:], pC[:], ACTF.Silu),
              reads=["pC"], writes=["vf"])
            for boff in (0, 8):
                for j in range(8):
                    E("tensor", lambda e, jj=j, bo=boff: e.matmul(
                        pA[:, bo:bo + 8], rap("nT", jj * 128, (jj + 1) * 128),
                        wabb[:, bo * 8 + jj * 8:bo * 8 + (jj + 1) * 8],
                        start=(jj == 0), stop=False),
                        reads=rres("nT") + ["wabb"], writes=["pA"])
                E("tensor", lambda e, bo=boff: e.matmul(
                    pA[:, bo:bo + 8], oneb[0:1, 0:128], brab[0:1, bo:bo + 8],
                    start=False, stop=True),
                    reads=["oneb", "brab"], writes=["pA"])
            E("scalar", lambda e: e.activation(smcol("ab", 16), pA[:, 0:16], ACTF.Sigmoid),
              reads=["pA"], writes=["sm_ab"])
            E("scalar", lambda e: e.activation(smcol("la", 8), smcol("ab", 16)[:, 0:8], ACTF.Ln),
              reads=["sm_ab"], writes=["sm_la"])
            for h in range(8):
                E("tensor", lambda e, hh=h: e.matmul(
                    pB[0:1, hh * 128:(hh + 1) * 128], smcol("la", 8)[:, hh:hh + 1],
                    tri[:], start=True, stop=True),
                    reads=["sm_la", "c_tri"], writes=["pB"])
            E("vector", lambda e: e.tensor_copy(lT1[:], pB[0:1, 0:1024]),
              reads=["pB"], writes=["lT1"])
            E("tensor", lambda e: e.matmul(pC[:, 0:8], tri[:], smcol("la", 8),
                                           start=True, stop=True),
              reads=["sm_la", "c_tri"], writes=["pC"])
            E("tensor", lambda e: e.matmul(pC[:, 8:16], sgt[:], smcol("la", 8),
                                           start=True, stop=True),
              reads=["sm_la", "c_sgt"], writes=["pC"])
            E("scalar", lambda e, cc=c: e.activation(smcol(f"gC{c % 2}", 8), pC[:, 0:8], ACTF.Exp)
              if False else e.activation(smcol("gam", 8), pC[:, 0:8], ACTF.Exp),
              reads=["pC"], writes=["sm_gam"])
            E("scalar", lambda e: e.activation(smcol("wde", 8), pC[:, 8:16], ACTF.Exp),
              reads=["pC"], writes=["sm_wde"])
            E("vector", lambda e, pr=par: e.tensor_tensor(
                smcol("gC0" if pr == 0 else "gC1", 8), smcol("gam", 8),
                smcol("wde", 8), ALU.mult),
                reads=["sm_gam", "sm_wde"], writes=[f"sm_gC{par}"])
            for (src, sres, nmm) in ((qf, "qf", "rq"), (kf, "kf", "rk")):
                E("scalar", lambda e, s=src: e.activation(scr[:], s[:], ACTF.Square),
                  reads=[sres, "scr"], writes=["scr"])
                E("vector", lambda e, n_=nmm: e.tensor_reduce(
                    smcol(n_, 8), scr[:].rearrange("p (h k) -> p h k", k=128),
                    AX, ALU.add),
                    reads=["scr"], writes=["sm_" + nmm])
                E("scalar", lambda e, n_=nmm: e.activation(
                    smcol(n_, 8), smcol(n_, 8), ACTF.Sqrt, bias=1e-6),
                    reads=["sm_" + nmm, "cst"], writes=["sm_" + nmm])
                E("vector", lambda e, n_=nmm: e.reciprocal(smcol(n_, 8), smcol(n_, 8)),
                  reads=["sm_" + nmm], writes=["sm_" + nmm])
            gcres = f"sm_gC{par}"
            gccol = smcol("gC0" if par == 0 else "gC1", 8)
            E("vector", lambda e, g=gccol: e.tensor_tensor(
                smcol("rgam", 8), smcol("rq", 8), smcol("gam", 8), ALU.mult),
                reads=["sm_rq", "sm_gam"], writes=["sm_rgam"])
            E("vector", lambda e: e.scalar_tensor_tensor(
                smcol("rkb", 8), smcol("rk", 8), -1.0, smcol("ab", 16)[:, 8:16],
                ALU.mult, ALU.mult),
                reads=["sm_rk", "sm_ab"], writes=["sm_rkb"])
            E("vector", lambda e: e.tensor_tensor(smcol("rkw", 8), smcol("rk", 8),
                                                  smcol("wde", 8), ALU.mult),
              reads=["sm_rk", "sm_wde"], writes=["sm_rkw"])
            E("vector", lambda e: e.scalar_tensor_tensor(
                smcol("bgn", 8), smcol("ab", 16)[:, 8:16], -1.0, smcol("gam", 8),
                ALU.mult, ALU.mult),
                reads=["sm_ab", "sm_gam"], writes=["sm_bgn"])
            E("vector", lambda e: e.tensor_tensor(smcol("rkbg", 8), smcol("rk", 8),
                                                  smcol("bgn", 8), ALU.mult),
              reads=["sm_rk", "sm_bgn"], writes=["sm_rkbg"])

            def cast8(dstname, src, sres, colap, colres):
                E("vector", lambda e: e.tensor_tensor(
                    rap(dstname).rearrange("p (h f) -> p h f", f=128),
                    src[:].rearrange("p (h f) -> p h f", f=128),
                    colap.rearrange("p (a b) -> p a b", b=1).broadcast_to([128, 8, 128]),
                    ALU.mult),
                    reads=[sres, colres] + rres(dstname), writes=rres(dstname))

            def dmaT8(dstname, srcname):
                for h in range(8):
                    E("tensor", lambda e, hh=h, sn=srcname: e.transpose(
                        pCb[:, hh * 128:(hh + 1) * 128],
                        hsl(rap(sn), hh), idb[:]),
                        reads=rres(srcname) + ["c_idb"], writes=["pC"])
                E("scalar", lambda e, dn=dstname: e.copy(rap(dn), pCb[:, 0:1024]),
                  reads=["pC"] + rres(dstname), writes=rres(dstname))

            qgT = f"qgT{par}"
            ktl = f"ktl{par}"
            cast8("qsc", qf, "qf", smcol("rq", 8), "sm_rq")
            dmaT8("qT", "qsc")
            cast8("qsc", qf, "qf", smcol("rgam", 8), "sm_rgam")
            dmaT8(qgT, "qsc")
            cast8("qsc", kf, "kf", smcol("rk", 8), "sm_rk")
            dmaT8("kT", "qsc")
            cast8("ksc", kf, "kf", smcol("rkb", 8), "sm_rkb")
            dmaT8("kpT", "ksc")
            cast8(ktl, kf, "kf", smcol("rkw", 8), "sm_rkw")
            for h in range(8):
                E("tensor", lambda e, hh=h: e.matmul(
                    psl(pA, hh), lT1[0:1, hh * 128:(hh + 1) * 128], onesf[:],
                    start=True, stop=False),
                    reads=["lT1", "onesf"], writes=["pA"])
                E("tensor", lambda e, hh=h: e.matmul(
                    psl(pA, hh), negof[:], lT1[0:1, hh * 128:(hh + 1) * 128],
                    start=False, stop=True),
                    reads=["lT1", "negof"], writes=["pA"])
                E("tensor", lambda e, hh=h: e.matmul(
                    psl(pB, hh), onesf[:], lT1[0:1, hh * 128:(hh + 1) * 128],
                    start=True, stop=False),
                    reads=["lT1", "onesf"], writes=["pB"])
                E("tensor", lambda e, hh=h: e.matmul(
                    psl(pB, hh), lT1[0:1, hh * 128:(hh + 1) * 128], negof[:],
                    start=False, stop=True),
                    reads=["lT1", "negof"], writes=["pB"])
            lm_b = lmts[:].rearrange("p (o f) -> p o f", o=1).broadcast_to([128, 8, 128])
            lm_b2 = lmst[:].rearrange("p (o f) -> p o f", o=1).broadcast_to([128, 8, 128])
            id_b8 = idb[:].rearrange("p (o f) -> p o f", o=1).broadcast_to([128, 8, 128])
            E("vector", lambda e: e.tensor_tensor(
                scr[:].rearrange("p (h f) -> p h f", f=128),
                pA[:].rearrange("p (h f) -> p h f", f=128), lm_b, ALU.add),
                reads=["pA", "c_lmts", "scr"], writes=["scr"])
            E("scalar", lambda e: e.activation(rap("Gs"), scr[:], ACTF.Exp),
              reads=["scr"], writes=rres("Gs"))
            E("vector", lambda e: e.tensor_tensor(
                scr[:].rearrange("p (h f) -> p h f", f=128),
                pB[:].rearrange("p (h f) -> p h f", f=128), lm_b2, ALU.add),
                reads=["pB", "c_lmst", "scr"], writes=["scr"])
            E("scalar", lambda e: e.activation(rap("GsT"), scr[:], ACTF.Exp),
              reads=["scr"], writes=rres("GsT"))
            E("vector", lambda e: e.tensor_tensor(
                rap("GiT").rearrange("p (h f) -> p h f", f=128),
                rap("GsT").rearrange("p (h f) -> p h f", f=128), id_b8, ALU.add),
                reads=rres("GsT") + ["c_idb"], writes=rres("GiT"))
            for h in range(8):
                E("tensor", lambda e, hh=h: e.matmul(
                    psl(pA, hh), hsl(rap("kpT"), hh), hsl(rap("kT"), hh),
                    start=True, stop=True),
                    reads=rres("kpT") + rres("kT"), writes=["pA"])
                E("tensor", lambda e, hh=h: e.matmul(
                    psl(pB, hh), hsl(rap("kT"), hh), hsl(rap("kpT"), hh),
                    start=True, stop=True),
                    reads=rres("kpT") + rres("kT"), writes=["pB"])
                E("tensor", lambda e, hh=h: e.matmul(
                    psl(pC, hh), hsl(rap("kT"), hh), hsl(rap("qT"), hh),
                    start=True, stop=True),
                    reads=rres("kT") + rres("qT"), writes=["pC"])
            bt = f"BT{par}"
            E("vector", lambda e, b=bt: e.tensor_tensor(rap(b), pC[:], rap("GiT"), ALU.mult),
              reads=["pC"] + rres("GiT") + rres(bt), writes=rres(bt))
            E("vector", lambda e: e.tensor_tensor(rap("X0"), pA[:], rap("Gs"), ALU.mult),
              reads=["pA"] + rres("Gs") + rres("X0"), writes=rres("X0"))
            E("vector", lambda e: e.tensor_tensor(
                rap("X0").rearrange("p (h f) -> p h f", f=128),
                rap("X0").rearrange("p (h f) -> p h f", f=128), id_b8, ALU.add),
                reads=rres("X0") + ["c_idb"], writes=rres("X0"))
            E("vector", lambda e: e.tensor_tensor(rap("T1b"), pB[:], rap("GsT"), ALU.mult),
              reads=["pB"] + rres("GsT") + rres("T1b"), writes=rres("T1b"))
            E("vector", lambda e: e.tensor_tensor(
                rap("Xp0").rearrange("p (h f) -> p h f", f=128),
                rap("T1b").rearrange("p (h f) -> p h f", f=128), id_b8, ALU.add),
                reads=rres("T1b") + ["c_idb"] + rres("Xp0"), writes=rres("Xp0"))
            E("vector", lambda e: e.scalar_tensor_tensor(
                rap("Gp").rearrange("p (h f) -> p h f", f=128),
                rap("T1b").rearrange("p (h f) -> p h f", f=128), -1.0, id_b8,
                ALU.mult, ALU.add),
                reads=rres("T1b") + ["c_idb"] + rres("Gp"), writes=rres("Gp"))
            E("vector", lambda e: e.scalar_tensor_tensor(
                rap("Gn").rearrange("p (h f) -> p h f", f=128),
                rap("X0").rearrange("p (h f) -> p h f", f=128), -1.0,
                twoi[:].rearrange("p (o f) -> p o f", o=1).broadcast_to([128, 8, 128]),
                ALU.mult, ALU.add),
                reads=rres("X0") + ["c_2i"] + rres("Gn"), writes=rres("Gn"))
            xcur, xpcur = "X0", "Xp0"
            for it in range(NEWTON_ITERS):
                for h in range(8):
                    E("tensor", lambda e, hh=h, xc=xcur: e.matmul(
                        psl(pA, hh), hsl(rap("Gp"), hh), hsl(rap(xc), hh),
                        start=True, stop=True),
                        reads=rres("Gp") + rres(xcur), writes=["pA"])
                    E("tensor", lambda e, hh=h, xpc=xpcur: e.matmul(
                        psl(pB, hh), hsl(rap("Gn"), hh), hsl(rap(xpc), hh),
                        start=True, stop=True),
                        reads=rres("Gn") + rres(xpcur), writes=["pB"])
                E("scalar", lambda e: e.copy(rap("T1b"), pA[:]),
                  reads=["pA"] + rres("T1b"), writes=rres("T1b"))
                E("vector", lambda e: e.tensor_copy(rap("T1p"), pB[:]),
                  reads=["pB"] + rres("T1p"), writes=rres("T1p"))
                xnext = "X1" if xcur == "X0" else "X0"
                xpnext = "Xp1" if xpcur == "Xp0" else "Xp0"
                for h in range(8):
                    E("tensor", lambda e, hh=h, xpc=xpcur: e.matmul(
                        psl(pA, hh), hsl(rap(xpc), hh), hsl(rap("T1b"), hh),
                        start=True, stop=False),
                        reads=rres(xpcur) + rres("T1b"), writes=["pA"])
                    E("tensor", lambda e, hh=h, xpc=xpcur: e.matmul(
                        psl(pA, hh), hsl(rap(xpc), hh), n2i[:],
                        start=False, stop=True),
                        reads=rres(xpcur) + ["c_n2i"], writes=["pA"])
                    E("tensor", lambda e, hh=h, xc=xcur: e.matmul(
                        psl(pB, hh), hsl(rap(xc), hh), hsl(rap("T1p"), hh),
                        start=True, stop=False),
                        reads=rres(xcur) + rres("T1p"), writes=["pB"])
                    E("tensor", lambda e, hh=h, xc=xcur: e.matmul(
                        psl(pB, hh), hsl(rap(xc), hh), n2i[:],
                        start=False, stop=True),
                        reads=rres(xcur) + ["c_n2i"], writes=["pB"])
                E("scalar", lambda e, xn=xnext: e.mul(rap(xn), pA[:], -1.0),
                  reads=["pA"] + rres(xnext), writes=rres(xnext))
                E("vector", lambda e, xpn=xpnext: e.tensor_scalar(
                    rap(xpn), pB[:], -1.0, None, ALU.mult),
                    reads=["pB"] + rres(xpnext), writes=rres(xpnext))
                xcur, xpcur = xnext, xpnext
            mbv = f"Mbv{par}"
            g2t = f"G2T{par}"
            cast8("bvb", vf, "vf", smcol("ab", 16)[:, 8:16], "sm_ab")
            cast8("kgb", kf, "kf", smcol("rkbg", 8), "sm_rkbg")
            for h in range(8):
                E("tensor", lambda e, hh=h, xpc=xpcur: e.matmul(
                    psl(pC, hh), hsl(rap(xpc), hh), hsl(rap("bvb"), hh),
                    start=True, stop=True),
                    reads=rres(xpcur) + rres("bvb"), writes=["pC"])
            E("scalar", lambda e, m=mbv: e.copy(rap(m), pC[:]),
              reads=["pC"] + rres(mbv), writes=rres(mbv))
            for h in range(8):
                E("tensor", lambda e, hh=h, xpc=xpcur: e.matmul(
                    psl(pC, hh), hsl(rap("kgb"), hh), hsl(rap(xpc), hh),
                    start=True, stop=True),
                    reads=rres(xpcur) + rres("kgb"), writes=["pC"])
            E("scalar", lambda e, g=g2t: e.copy(rap(g), pC[:]),
              reads=["pC"] + rres(g2t), writes=rres(g2t))

        def stage2(c):
            par = c % 2
            qgT, ktl = f"qgT{par}", f"ktl{par}"
            mbv, g2t, bt = f"Mbv{par}", f"G2T{par}", f"BT{par}"
            gcres = f"sm_gC{par}"
            gccol = smcol("gC0" if par == 0 else "gC1", 8)
            xp_ap, xp_res = xpt_bufs[par]
            E("vector", lambda e: e.tensor_copy(rap("S0b"), Sst[:]),
              reads=["Sst"] + rres("S0b"), writes=rres("S0b"))
            for h in range(8):
                E("tensor", lambda e, hh=h, g=g2t: e.matmul(
                    psl(pD, hh), hsl(rap(g), hh), hsl(rap("S0b"), hh),
                    start=True, stop=False),
                    reads=rres(g2t) + rres("S0b"), writes=["pD_b0", "pD_b1"])
                E("tensor", lambda e, hh=h, m=mbv: e.matmul(
                    psl(pD, hh), idb[:], hsl(rap(m), hh),
                    start=False, stop=True),
                    reads=["c_idb"] + rres(mbv), writes=["pD_b0", "pD_b1"])
            E("scalar", lambda e: e.copy(rap("Ub"), pD[:]),
              reads=["pD_b0", "pD_b1"] + rres("Ub"), writes=rres("Ub"))
            for h in range(8):
                E("tensor", lambda e, hh=h, q=qgT: e.matmul(
                    psl(pD, hh), hsl(rap("S0b"), hh), hsl(rap(q), hh),
                    start=True, stop=False),
                    reads=rres("S0b") + rres(qgT), writes=["pD_b0", "pD_b1"])
                E("tensor", lambda e, hh=h, b=bt: e.matmul(
                    psl(pD, hh), hsl(rap("Ub"), hh), hsl(rap(b), hh),
                    start=False, stop=True),
                    reads=rres("Ub") + rres(bt), writes=["pD_b0", "pD_b1"])
            E("scalar", lambda e: e.copy(rap("oTb"), pD[:]),
              reads=["pD_b0", "pD_b1"] + rres("oTb"), writes=rres("oTb"))
            if DEBUG_DUMP:
                E("gpsimd", lambda e, cc=c: e.dma_start(out=d["dbgU_h"][cc], in_=rap("Ub")),
                  reads=rres("Ub"), writes=["dbgU"], dma=True)
                E("gpsimd", lambda e, cc=c: e.dma_start(out=d["dbgO_h"][cc], in_=rap("oTb")),
                  reads=rres("oTb"), writes=["dbgO"], dma=True)
                E("gpsimd", lambda e, cc=c, m=mbv: e.dma_start(out=d["dbgM_h"][cc], in_=rap(m)),
                  reads=rres(mbv), writes=["dbgM"], dma=True)
                E("gpsimd", lambda e, cc=c, g=g2t: e.dma_start(out=d["dbgG_h"][cc], in_=rap(g)),
                  reads=rres(g2t), writes=["dbgG"], dma=True)
            E("vector", lambda e, g=gccol: e.tensor_tensor(
                Sst[:].rearrange("p (h f) -> p h f", f=128),
                Sst[:].rearrange("p (h f) -> p h f", f=128),
                g.rearrange("p (a b) -> p a b", b=1).broadcast_to([128, 8, 128]),
                ALU.mult),
                reads=["Sst", gcres], writes=["Sst"])
            for h in range(8):
                E("tensor", lambda e, hh=h, k=ktl: e.matmul(
                    psl(pD, hh), hsl(rap(k), hh), hsl(rap("Ub"), hh),
                    start=True, stop=True),
                    reads=rres(ktl) + rres("Ub"), writes=["pD_b0", "pD_b1"])
            E("vector", lambda e: e.scalar_tensor_tensor(
                Sst[:], pD[:], 1.0, Sst[:], ALU.mult, ALU.add),
                reads=["pD_b0", "pD_b1", "Sst"], writes=["Sst"])
            for o in range(0, 1024, 512):
                for j in range(8):
                    E("tensor", lambda e, jj=j, o0=o: e.matmul(
                        pD[:, o0:o0 + 512], hsl(rap("oTb"), jj),
                        wob[:, jj * 1024 + o0:jj * 1024 + o0 + 512],
                        start=(jj == 0), stop=(jj == 7)),
                        reads=rres("oTb") + ["wob"], writes=pDres(o, o + 512))
            E("vector", lambda e, xp=xp_ap: e.scalar_tensor_tensor(
                xp[:], pD[:], 1.0, xp[:], ALU.mult, ALU.add),
                reads=["pD_b0", "pD_b1"] + xp_res, writes=xp_res)
            E("sync", lambda e, cc=c, xp=xp_ap: e.dma_start(
                out=d["xa_h"][cc * 128:(cc + 1) * 128, :], in_=xp[:]),
                reads=xp_res, writes=["xa_h"], dma=True)

        for c in range(NCH + 1):
            if c < NCH:
                stage1(c)
            if c >= 1:
                stage2(c - 1)

        for g in range(8):
            for j in range(8):
                buf, bres = wld[j % 2], f"x0t{j % 2}"
                E("gpsimd", lambda e, gg=g, jj=j, b=buf: e.dma_start(
                    out=b[:, 0:512],
                    in_=d["W1"][jj * 128:(jj + 1) * 128, gg * 512:(gg + 1) * 512]),
                    writes=[bres], dma=True)
                E("tensor", lambda e, jj=j, b=buf: e.matmul(
                    pD[0:1, 0:512], lnc[:, 24 + jj:25 + jj], b[:, 0:512],
                    start=(jj == 0), stop=(jj == 7)),
                    reads=[bres, "lnc"], writes=["pD_b0"])
                E("vector", lambda e, b=buf, jj=j: e.tensor_scalar(
                    wcast[:, 0:512], b[:, 0:512], lnc[:, 16 + jj:17 + jj],
                    None, ALU.mult),
                    reads=[bres, "lnc", "wcast"], writes=["wcast"])
                E("gpsimd", lambda e, gg=g, jj=j: e.dma_start(
                    out=d["w1e_h"][gg, :, jj, :], in_=wcast[:, 0:512]),
                    reads=["wcast"], writes=["w1e_h"], dma=True)
            E("gpsimd", lambda e, gg=g: e.dma_start(
                out=b1s[:], in_=d["b1"][gg * 512:(gg + 1) * 512].rearrange("(o x) -> o x", o=1)),
                writes=["b1s"], dma=True)
            E("vector", lambda e, gg=g: e.scalar_tensor_tensor(
                br1[0:1, gg * 512:(gg + 1) * 512], pD[0:1, 0:512], 1.0,
                b1s[0:1, :], ALU.mult, ALU.add),
                reads=["pD_b0", "b1s"], writes=["br1"])

        deq_lists = {}

        def dequant_emit(src, dst, nchunks, width, nblk):
            nj = width // 128
            lst = deq_lists.setdefault(dst, [])

            def blkA(pc, gi, src=src, dst=dst, width=width, nblk=nblk, nj=nj):
                buf, bres = stg[gi % 2], [f"stg{gi % 2}_{k}" for k in range(8)]
                E("gpsimd", lambda e, pcc=pc, b=buf: e.dma_start(
                    out=b[:, 0:width], in_=d[src][pcc * 128:(pcc + 1) * 128, :]),
                    writes=bres, dma=True)
                E("vector", lambda e, b=buf: e.tensor_reduce(
                    smcol("bmx", 96)[:, 0:nblk],
                    b[:, 0:width].rearrange("p (n k) -> p n k", k=32), AX,
                    ALU.max, apply_absolute_value=True),
                    reads=bres, writes=["sm_bmx"])
                E("vector", lambda e: e.tensor_scalar(
                    smcol("bs", 96)[:, 0:nblk], smcol("bmx", 96)[:, 0:nblk],
                    1.0 / 7.0, 1e-12, ALU.mult, ALU.max),
                    reads=["sm_bmx"], writes=["sm_bs"])
                E("vector", lambda e: e.reciprocal(
                    smcol("bi", 96)[:, 0:nblk], smcol("bs", 96)[:, 0:nblk]),
                    reads=["sm_bs"], writes=["sm_bi"])
                bi_b = smcol("bi", 96)[:, 0:nblk].rearrange("p (n o) -> p n o", o=1).broadcast_to([128, nblk, 32])
                bs_b = smcol("bs", 96)[:, 0:nblk].rearrange("p (n o) -> p n o", o=1).broadcast_to([128, nblk, 32])
                E("vector", lambda e, b=buf: e.tensor_tensor(
                    b[:, 0:width].rearrange("p (n k) -> p n k", k=32),
                    b[:, 0:width].rearrange("p (n k) -> p n k", k=32), bi_b, ALU.mult),
                    reads=bres + ["sm_bi"], writes=bres)
                E("vector", lambda e, b=buf: e.tensor_scalar(
                    b[:, 0:width], b[:, 0:width], MAGIC, MAGIC, ALU.add, ALU.subtract),
                    reads=bres, writes=bres)
                E("vector", lambda e, b=buf: e.tensor_scalar(
                    b[:, 0:width], b[:, 0:width], -8.0, 7.0, ALU.max, ALU.min),
                    reads=bres, writes=bres)
                E("vector", lambda e, b=buf: e.tensor_tensor(
                    b[:, 0:width].rearrange("p (n k) -> p n k", k=32),
                    b[:, 0:width].rearrange("p (n k) -> p n k", k=32), bs_b, ALU.mult),
                    reads=bres + ["sm_bs"], writes=bres)

            def blkB(pc, gi, src=src, dst=dst, width=width, nblk=nblk, nj=nj):
                buf, bres = stg[gi % 2], [f"stg{gi % 2}_{k}" for k in range(8)]
                DQALL = [f"dq_s{k}" for k in range(8)]
                for w0 in range(0, nj, 8):
                    for j in range(w0, min(w0 + 8, nj)):
                        sl = j % 8
                        E("tensor", lambda e, jj=j, s=sl, b=buf: e.transpose(
                            pC[:, s * 128:(s + 1) * 128],
                            b[:, jj * 128:(jj + 1) * 128], idf[:]),
                            reads=[bres[j // 4], "c_idf"], writes=["pC"])
                    E("scalar", lambda e: e.copy(rb[:, 0:1024], pC[:, 0:1024]),
                      reads=["pC"], writes=DQALL)
                    for j in range(w0, min(w0 + 8, nj)):
                        sl = j % 8
                        if dst == "dnT_h":
                            E("sync", lambda e, pcc=pc, jj=j, s=sl: e.dma_start(
                                out=d["dnT_h"][jj, :, pcc * 128:(pcc + 1) * 128],
                                in_=rb[:, s * 128:(s + 1) * 128]),
                                reads=[f"dq_s{sl}"], writes=["dnT_h"], dma=True)
                        else:
                            E("sync", lambda e, pcc=pc, jj=j, s=sl, dn=dst: e.dma_start(
                                out=d[dn][pcc, :, jj * 128:(jj + 1) * 128],
                                in_=rb[:, s * 128:(s + 1) * 128]),
                                reads=[f"dq_s{sl}"], writes=[dst], dma=True)

            for pc in range(nchunks):
                lst.append((pc, blkA, blkB))

        dequant_emit("wup", "upT_h", 24, 1024, 32)
        dequant_emit("wgt", "gtT_h", 24, 1024, 32)
        dequant_emit("wdn", "dnT_h", 8, 3072, 96)
        deq_pairs = []
        _dn = deq_lists["dnT_h"]
        for k in range(24):
            srcs = [deq_lists["upT_h"][k], deq_lists["gtT_h"][k]]
            if k % 3 == 2:
                srcs.append(_dn[k // 3])
            for pc, fa, fb in srcs:
                g = len(deq_pairs)
                deq_pairs.append((lambda p_=pc, f_=fa, g_=g: f_(p_, g_),
                                  lambda p_=pc, f_=fb, g_=g: f_(p_, g_)))
        deq_pos = [0, 0]

        def deq_step():
            if deq_pos[0] < len(deq_pairs):
                deq_pairs[deq_pos[0]][0]()
                deq_pos[0] += 1
            while deq_pos[1] < deq_pos[0] - 1:
                deq_pairs[deq_pos[1]][1]()
                deq_pos[1] += 1

        def deq_drain():
            while deq_pos[1] < len(deq_pairs):
                if deq_pos[0] < len(deq_pairs):
                    deq_pairs[deq_pos[0]][0]()
                    deq_pos[0] += 1
                deq_pairs[deq_pos[1]][1]()
                deq_pos[1] += 1

        sch.barrier_all()
        for j2 in range(16):
            cb = wob if j2 < 8 else wvb
            sl2 = j2 % 8
            E("gpsimd", lambda e, jj=j2, b=cb[:, sl2 * 1024:(sl2 + 1) * 1024]:
              e.dma_start(out=b, in_=d["w2b_h"][jj]),
                writes=[f"w2c{j2}"], dma=True)
        w1ld = [(wqb[:, 0:4096], ["w1s0"]), (wqb[:, 4096:8192], ["w1s1"])]
        w2ld = [(wkb[:, 0:1024], ["w2s0"]), (wkb[:, 1024:2048], ["w2s1"])]
        def a2_cs(tt):
            for cs in range(4):
                c = tt * 4 + cs
                xb, xbres = x0t[cs % 2], f"x0t{cs % 2}"
                E("gpsimd", lambda e, cc=c, b=xb: e.dma_start(
                    out=b[:], in_=d["xa_h"][cc * 128:(cc + 1) * 128, :]),
                    reads=["xa_h"], writes=[xbres], dma=True)
                m2, r2, m2r, r2r = stats_ln(xb[:], xbres, "2")
                E("vector", lambda e, b=xb: e.tensor_scalar(
                    rap("nb"), b[:], m2, r2, ALU.subtract, ALU.mult),
                    reads=[xbres, m2r, r2r], writes=rres("nb"))
                for j in range(8):
                    E("sync",
                      lambda e, jj=j, css=cs: e.dma_start_transpose(
                        rb[:, A2_HT + jj * 512 + css * 128:A2_HT + jj * 512 + (css + 1) * 128],
                        rap("nb", jj * 128, (jj + 1) * 128)),
                        reads=rres("nb"),
                        writes=rcols(A2_HT + j * 512 + cs * 128,
                                     A2_HT + j * 512 + (cs + 1) * 128), dma=True)

        def a2_w1(tt):
            for g in range(8):
                buf, bufres = w1ld[g % 2]
                for hf in range(2):
                    E("gpsimd", lambda e, gg=g, b=buf, h_=hf: e.dma_start(
                        out=b[:, h_ * 2048:(h_ + 1) * 2048].rearrange(
                            "p (j f) -> p j f", f=512),
                        in_=d["w1e_h"][gg, :, h_ * 4:(h_ + 1) * 4, :]),
                        writes=bufres, dma=True)
                for oc4 in range(4):
                    pp, ppres = (pA, "pA") if oc4 % 2 == 0 else (pB, "pB")
                    for j in range(8):
                        E("tensor", lambda e, jj=j, o4=oc4, b=buf, _p=pp: e.matmul(
                            _p[:, 0:512],
                            b[:, jj * 512 + o4 * 128:jj * 512 + (o4 + 1) * 128],
                            rb[:, A2_HT + jj * 512:A2_HT + (jj + 1) * 512],
                            start=(jj == 0), stop=False),
                            reads=bufres + rcols(A2_HT + j * 512, A2_HT + (j + 1) * 512),
                            writes=[ppres])
                    oc = g * 4 + oc4
                    E("tensor", lambda e, occ=oc, _p=pp: e.matmul(
                        _p[:, 0:512], br1[0:1, occ * 128:(occ + 1) * 128],
                        oneb[0:1, 0:512], start=False, stop=True),
                        reads=["br1", "oneb"], writes=[ppres])
                    E("scalar", lambda e, occ=oc, _p=pp: e.activation(
                        rb[:, A2_GT + occ * 512:A2_GT + (occ + 1) * 512],
                        _p[:, 0:512], ACTF.Gelu),
                        reads=[ppres],
                        writes=rcols(A2_GT + oc * 512, A2_GT + (oc + 1) * 512))
                deq_step()

        def a2_w2(tt):
            for j2 in range(16, 20):
                sl2 = j2 % 8
                E("gpsimd", lambda e, jj=j2, b=wkb[:, sl2 * 1024:(sl2 + 1) * 1024]:
                  e.dma_start(out=b, in_=d["w2b_h"][jj]),
                    writes=[f"w2s{sl2}"], dma=True)
            for j in range(32):
                if j < 8:
                    buf, bufres = wob[:, (j % 8) * 1024:(j % 8 + 1) * 1024], [f"w2c{j}"]
                elif j < 16:
                    buf, bufres = wvb[:, (j % 8) * 1024:(j % 8 + 1) * 1024], [f"w2c{j}"]
                else:
                    if j + 4 < 32:
                        jn2, sn2 = j + 4, (j + 4) % 8
                        E("gpsimd", lambda e, jj=jn2, b=wkb[:, sn2 * 1024:(sn2 + 1) * 1024]:
                          e.dma_start(out=b, in_=d["w2b_h"][jj]),
                            writes=[f"w2s{sn2}"], dma=True)
                    buf, bufres = wkb[:, (j % 8) * 1024:(j % 8 + 1) * 1024], [f"w2s{j % 8}"]
                for t4 in range(4):
                    for o in range(0, 1024, 512):
                        E("tensor", lambda e, jj=j, _t=t4, o0=o, b=buf: e.matmul(
                            p[_t][:, o0:o0 + 512],
                            rb[:, A2_GT + jj * 512 + _t * 128:A2_GT + jj * 512 + (_t + 1) * 128],
                            b[:, o0:o0 + 512], start=(jj == 0), stop=False),
                            reads=bufres + rcols(A2_GT + j * 512, A2_GT + (j + 1) * 512),
                            writes=([("pA", "pB", "pC")[t4]] if t4 < 3 else pDres(o, o + 512)))
            for t4 in range(4):
                for o in range(0, 1024, 512):
                    E("tensor", lambda e, _t=t4, o0=o: e.matmul(
                        p[_t][:, o0:o0 + 512], oneb[0:1, 0:128], br2[0:1, o0:o0 + 512],
                        start=False, stop=True),
                        reads=["oneb", "br2"],
                        writes=([("pA", "pB", "pC")[t4]] if t4 < 3 else pDres(o, o + 512)))

        def a2_x1(tt):
            for t4 in range(4):
                c = tt * 4 + t4
                pres = [("pA", "pB", "pC")[t4]] if t4 < 3 else ["pD_b0", "pD_b1"]
                E("gpsimd", lambda e, cc=c: e.dma_start(
                    out=qf[:], in_=d["xa_h"][cc * 128:(cc + 1) * 128, :]),
                    reads=["xa_h"], writes=["qf"], dma=True)
                E("gpsimd", lambda e, cc=c: e.dma_start(
                    out=kf[:], in_=d["x0"][cc * 128:(cc + 1) * 128, :]),
                    writes=["kf"], dma=True)
                E("vector", lambda e, _t=t4: e.scalar_tensor_tensor(
                    vf[:], p[_t][:], 1.0, qf[:], ALU.mult, ALU.add),
                    reads=pres + ["qf", "vf"], writes=["vf"])
                E("vector", lambda e: e.tensor_tensor(vf[:], vf[:], kf[:], ALU.add),
                  reads=["vf", "kf"], writes=["vf"])
                E("sync", lambda e, cc=c: e.dma_start(
                    out=d["x1_h"][cc * 128:(cc + 1) * 128, :], in_=vf[:]),
                    reads=["vf"], writes=["x1_h"], dma=True)
                m3, r3, m3r, r3r = stats_ln(vf[:], "vf", "3")
                E("vector", lambda e, cc=c: e.tensor_copy(ystat[:, cc:cc + 1], m3),
                  reads=[m3r], writes=["ystat_m"])
                E("vector", lambda e, cc=c: e.tensor_copy(ystat[:, 32 + cc:33 + cc], r3),
                  reads=[r3r], writes=["ystat_r"])
                E("vector", lambda e: e.tensor_scalar(
                    scr[:], vf[:], m3, r3, ALU.subtract, ALU.mult),
                    reads=["vf", m3r, r3r, "scr"], writes=["scr"])
                E("vector", lambda e: e.tensor_reduce(
                    smcol("ymx"), scr[:], AX, ALU.max, apply_absolute_value=True),
                    reads=["scr"], writes=["sm_ymx"])
                E("vector", lambda e: e.tensor_tensor(
                    ystat[:, 64:65], ystat[:, 64:65], smcol("ymx"), ALU.max),
                    reads=["sm_ymx", "ystat_ymx"], writes=["ystat_ymx"])

        a2_cs(0)
        for tt in range(NTT):
            a2_w1(tt)
            a2_w2(tt)
            if tt + 1 < NTT:
                a2_cs(tt + 1)
            a2_x1(tt)

        deq_drain()

        def allreduce_scalar(k, srccols, outcols):
            for i, (ap_, res_) in enumerate(srccols):
                E("tensor", lambda e, a=ap_, ii=i: e.transpose(
                    pD[0:1, 512 + ii * 128:512 + (ii + 1) * 128], a, idf[:]),
                    reads=[res_, "c_idf"], writes=["pD_b1"])
            for i in range(len(srccols)):
                E("vector", lambda e, ii=i: e.tensor_reduce(
                    smcol("ccv", 2)[0:1, ii:ii + 1],
                    pD[0:1, 512 + ii * 128:512 + (ii + 1) * 128], AX, ALU.max),
                    reads=["pD_b1"], writes=["sm_ccv"])
            E("gpsimd", lambda e, kk=k: e.dma_start(
                out=d[f"ccb{kk}"], in_=smcol("ccv", 2)[0:1, :]),
                reads=["sm_ccv"], writes=[f"ccb{k}"], dma=True)
            E("gpsimd", lambda e, kk=k: e.collective_compute(
                "AllReduce", ALU.max, replica_groups=[list(range(8))],
                ins=[d[f"ccb{kk}"].opt()], outs=[d[f"ccr{kk}"].opt()]),
                reads=[f"ccb{k}"], writes=[f"ccr{k}"])
            E("gpsimd", lambda e, kk=k: e.dma_start(
                out=smcol("ccv", 2)[0:1, :], in_=d[f"ccr{kk}"]),
                reads=[f"ccr{k}"], writes=["sm_ccv"], dma=True)
            for i, (ap_, res_) in enumerate(outcols):
                E("tensor", lambda e, ii=i: e.matmul(
                    pD[:, 512 + ii:513 + ii], onesf[:], smcol("ccv", 2)[0:1, ii:ii + 1],
                    start=True, stop=True),
                    reads=["sm_ccv", "onesf"], writes=["pD_b1"])
                E("vector", lambda e, a=ap_, ii=i: e.tensor_copy(
                    a, pD[:, 512 + ii:513 + ii]),
                    reads=["pD_b1"], writes=[res_])

        sch.barrier_all()
        allreduce_scalar(0, [(ystat[:, 64:65], "ystat_ymx")],
                         [(ystat[:, 65:66], "ystat_g")])
        E("vector", lambda e: e.tensor_scalar(
            ystat[:, 66:67], ystat[:, 65:66], 1.0 / 32767.0, 1e-12, ALU.mult, ALU.max),
            reads=["ystat_g"], writes=["ystat_sy"])
        E("vector", lambda e: e.reciprocal(ystat[:, 67:68], ystat[:, 66:67]),
          reads=["ystat_sy"], writes=["ystat_syi"])
        E("vector", lambda e: e.tensor_tensor(
            ystat[:, 32:32 + NCH], ystat[:, 32:32 + NCH],
            ystat[:, 67:68].broadcast_to([128, NCH]), ALU.mult),
            reads=["ystat_r", "ystat_syi"], writes=["ystat_r"])

        QTOK = min(1024, BW)
        NQ = BW // QTOK
        NCQ = QTOK // 128
        NMM2 = QTOK // 512
        YQ0 = [A2_HT, A2_HT + 8 * QTOK]

        def yprep_chunk(q, cs):
            c = q * NCQ + cs
            Y = YQ0[q % 2]
            xb, xbres = x0t[cs % 2], f"x0t{cs % 2}"
            E("gpsimd", lambda e, cc=c, b=xb: e.dma_start(
                out=b[:], in_=d["x1_h"][cc * 128:(cc + 1) * 128, :]),
                reads=["x1_h"], writes=[xbres], dma=True)
            E("vector", lambda e, cc=c, b=xb: e.tensor_scalar(
                scr[:], b[:], ystat[:, cc:cc + 1], ystat[:, 32 + cc:33 + cc],
                ALU.subtract, ALU.mult),
                reads=[xbres, "ystat_m", "ystat_r", "scr"], writes=["scr"])
            E("vector", lambda e: e.tensor_scalar(
                scr[:], scr[:], MAGIC, MAGIC, ALU.add, ALU.subtract),
                reads=["scr"], writes=["scr"])
            E("vector", lambda e: e.tensor_scalar(
                scr[:], scr[:], -32768.0, 32767.0, ALU.max, ALU.min),
                reads=["scr"], writes=["scr"])
            E("vector", lambda e: e.tensor_scalar(
                rap("nb"), scr[:], ystat[:, 66:67], None, ALU.mult),
                reads=["scr", "ystat_sy"], writes=rres("nb"))
            for j in range(8):
                E("sync", lambda e, jj=j, css=cs, Y_=Y: e.dma_start_transpose(
                    rb[:, Y_ + jj * QTOK + css * 128:Y_ + jj * QTOK + (css + 1) * 128],
                    rap("nb", jj * 128, (jj + 1) * 128)),
                    reads=rres("nb"),
                    writes=rcols(Y + j * QTOK + cs * 128,
                                 Y + j * QTOK + (cs + 1) * 128), dma=True)

        for cs0 in range(NCQ):
            yprep_chunk(0, cs0)
        WSLOT = 4
        upb = [wqb[:, sl * 1024:(sl + 1) * 1024] for sl in range(WSLOT)]
        gtb = [wkb[:, sl * 1024:(sl + 1) * 1024] for sl in range(WSLOT)]
        seq = [(q, oc, m) for q in range(NQ) for oc in range(24) for m in range(2)]
        PF = 4

        def emit_load(i):
            q_, oc_, m_ = seq[i]
            idx = q_ * 24 + oc_
            sl = idx % WSLOT
            b = (upb, gtb)[m_][sl]
            w = ("upT_h", "gtT_h")[m_]
            res = [f"{('ups', 'gts')[m_]}{sl}"]
            E("gpsimd", lambda e, occ=oc_, b=b, w=w: e.dma_start(
                out=b, in_=d[w][occ]),
                reads=[w], writes=res, dma=True)
            return b, res

        ldinfo = {}
        for i in range(min(PF, len(seq))):
            ldinfo[i] = emit_load(i)
        for i, (q, oc, m) in enumerate(seq):
            if i + PF < len(seq):
                ldinfo[i + PF] = emit_load(i + PF)
            wb, wres = ldinfo.pop(i)
            Y = YQ0[q % 2]
            yres = rcols(Y, Y + 8 * QTOK)
            idx = q * 24 + oc
            if m == 0:
                pp = pA if idx % 2 == 0 else pC
                pres_of = (lambda nn, nm=("pA" if idx % 2 == 0 else "pC"): [nm])
                sti, hb = 0, "rup_h"
            else:
                if idx % 2 == 0:
                    pp = pB
                    pres_of = (lambda nn: ["pB"])
                else:
                    pp = pD
                    pres_of = (lambda nn: pDres(nn * 512, (nn + 1) * 512))
                sti, hb = 1, "rgt_h"
            for j in range(8):
                for n in range(NMM2):
                    E("tensor", lambda e, jj=j, nn=n, b=wb, _p=pp, Y_=Y: e.matmul(
                        _p[:, nn * 512:(nn + 1) * 512],
                        b[:, jj * 128:(jj + 1) * 128],
                        rb[:, Y_ + jj * QTOK + nn * 512:Y_ + jj * QTOK + (nn + 1) * 512],
                        start=(jj == 0), stop=(jj == 7)),
                        reads=wres + rcols(Y + j * QTOK + n * 512,
                                           Y + j * QTOK + (n + 1) * 512),
                        writes=pres_of(n))
            so = (oc % 2) * NMM2
            sres = [f"stg{sti}_{so + n}" for n in range(NMM2)]
            for n in range(NMM2):
                E("scalar", lambda e, _p=pp, s=sti, nn=n, so_=so: e.copy(
                    stg[s][:, (so_ + nn) * 512:(so_ + nn + 1) * 512],
                    _p[:, nn * 512:(nn + 1) * 512]),
                    reads=pres_of(n), writes=[sres[n]])
            E("gpsimd", lambda e, occ=oc, s=sti, so_=so, h_=hb, qq=q: e.dma_start(
                out=d[h_][occ, :, qq * QTOK:(qq + 1) * QTOK],
                in_=stg[s][:, so_ * 512:(so_ + NMM2) * 512]),
                reads=sres, writes=[hb], dma=True)
            mcol = smcol("ymx2", 48)[:, sti * 24 + oc:sti * 24 + oc + 1]
            mres = f"sm_ymx2_{sti}_{oc}"
            if q == 0:
                E("vector", lambda e, s=sti, so_=so, mc=mcol: e.tensor_reduce(
                    mc, stg[s][:, so_ * 512:(so_ + NMM2) * 512], AX,
                    ALU.max, apply_absolute_value=True),
                    reads=sres, writes=[mres])
            else:
                tcol = smcol("ymx3", 48)[:, sti * 24 + oc:sti * 24 + oc + 1]
                tres = f"sm_ymx3_{sti}_{oc}"
                E("vector", lambda e, s=sti, so_=so, tc=tcol: e.tensor_reduce(
                    tc, stg[s][:, so_ * 512:(so_ + NMM2) * 512], AX,
                    ALU.max, apply_absolute_value=True),
                    reads=sres, writes=[tres])
                E("vector", lambda e, mc=mcol, tc=tcol: e.tensor_tensor(
                    mc, mc, tc, ALU.max),
                    reads=[tres, mres], writes=[mres])
            if m == 1 and q + 1 < NQ and oc < NCQ:
                yprep_chunk(q + 1, oc)

        E("vector", lambda e: e.tensor_reduce(
            ystat[:, 68:69], smcol("ymx2", 48)[:, 0:24], AX, ALU.max),
            reads=[f"sm_ymx2_0_{o}" for o in range(24)], writes=["ystat_um"])
        E("vector", lambda e: e.tensor_reduce(
            ystat[:, 69:70], smcol("ymx2", 48)[:, 24:48], AX, ALU.max),
            reads=[f"sm_ymx2_1_{o}" for o in range(24)], writes=["ystat_gm"])
        allreduce_scalar(1, [(ystat[:, 68:69], "ystat_um"), (ystat[:, 69:70], "ystat_gm")],
                         [(ystat[:, 70:71], "ystat_umg"), (ystat[:, 71:72], "ystat_gmg")])
        for (srcc, sres, oc_, ores) in ((70, "ystat_umg", 72, "ystat_su"),
                                        (71, "ystat_gmg", 74, "ystat_sg")):
            E("vector", lambda e, s=srcc, o=oc_: e.tensor_scalar(
                ystat[:, o:o + 1], ystat[:, s:s + 1], 1.0 / 32767.0, 1e-12,
                ALU.mult, ALU.max),
                reads=[sres], writes=[ores])
            E("vector", lambda e, o=oc_: e.reciprocal(ystat[:, o + 1:o + 2],
                                                      ystat[:, o:o + 1]),
              reads=[ores], writes=[ores + "i"])

        E("vector", lambda e: e.memset(ystat[:, 76:77], 0.0), writes=["ystat_tm"])
        ALLS = [f"stg0_{k}" for k in range(8)]
        ALLS1 = [f"stg1_{k}" for k in range(8)]
        sg2p = [rb[:, 8192:16384].bitcast(F32), rb[:, 16384:24576].bitcast(F32)]
        ALLS2p = [rcols(8192, 16384), rcols(16384, 24576)]
        def b2_ops(oc):
            ops = []

            def E2(eng, fn, **kw):
                ops.append(lambda e_=eng, f_=fn, k_=kw: E(e_, f_, **k_))
            bt = stg[oc % 2]
            sg2 = sg2p[oc % 2]
            ALLS2 = ALLS2p[oc % 2]
            btres = ALLS if oc % 2 == 0 else ALLS1
            E2("gpsimd", lambda e, occ=oc, b=bt: e.dma_start(
                out=b[:, 0:BW], in_=d["rgt_h"][occ]),
                reads=["rgt_h"], writes=btres, dma=True)
            E2("scalar", lambda e, b=bt: e.activation(
                b[:, 0:BW], b[:, 0:BW], ACTF.Identity, bias=MAGIC, scale=ystat[:, 75:76]),
                reads=btres + ["ystat_sgi", "cst"], writes=btres)
            E2("scalar", lambda e, b=bt: e.activation(
                b[:, 0:BW], b[:, 0:BW], ACTF.Identity, bias=-MAGIC),
                reads=btres + ["cst"], writes=btres)
            E2("vector", lambda e, b=bt: e.tensor_scalar(
                b[:, 0:BW], b[:, 0:BW], -32768.0, 32767.0, ALU.max, ALU.min),
                reads=btres, writes=btres)
            E2("scalar", lambda e, b=bt: e.activation(
                b[:, 0:BW], b[:, 0:BW], ACTF.Identity, scale=ystat[:, 74:75]),
                reads=btres + ["ystat_sg"], writes=btres)
            E2("scalar", lambda e, b=bt, s2=sg2: e.activation(s2[:, 0:BW], b[:, 0:BW], ACTF.Sigmoid),
              reads=btres + ALLS2, writes=ALLS2)
            E2("scalar", lambda e, s2=sg2: e.activation(
                s2[:, 0:BW], s2[:, 0:BW], ACTF.Identity, bias=MAGIC, scale=cst[:, 3:4]),
                reads=ALLS2 + ["cst"], writes=ALLS2)
            E2("vector", lambda e, s2=sg2: e.tensor_scalar(
                s2[:, 0:BW], s2[:, 0:BW], -MAGIC, 0.0, ALU.add, ALU.max),
                reads=ALLS2, writes=ALLS2)
            E2("vector", lambda e, s2=sg2: e.tensor_scalar(
                s2[:, 0:BW], s2[:, 0:BW], 65535.0, 1.0 / 65536.0, ALU.min, ALU.mult),
                reads=ALLS2, writes=ALLS2)
            E2("vector", lambda e, b=bt, s2=sg2: e.tensor_tensor(b[:, 0:BW], b[:, 0:BW], s2[:, 0:BW], ALU.mult),
              reads=btres + ALLS2, writes=btres)
            E2("vector", lambda e, b=bt: e.tensor_reduce(
                smcol("um2"), b[:, 0:BW], AX, ALU.max, apply_absolute_value=True),
                reads=btres, writes=["sm_um2"])
            E2("vector", lambda e: e.tensor_tensor(
                ystat[:, 76:77], ystat[:, 76:77], smcol("um2"), ALU.max),
                reads=["sm_um2", "ystat_tm"], writes=["ystat_tm"])
            E2("gpsimd", lambda e, occ=oc, b=bt: e.dma_start(
                out=d["tt_h"][occ], in_=b[:, 0:BW]),
                reads=btres, writes=["tt_h"], dma=True)

            return ops

        import itertools as _it
        for k2 in range(12):
            for a_, b_ in _it.zip_longest(b2_ops(2 * k2), b2_ops(2 * k2 + 1)):
                if a_ is not None:
                    a_()
                if b_ is not None:
                    b_()

        allreduce_scalar(2, [(ystat[:, 76:77], "ystat_tm")],
                         [(ystat[:, 77:78], "ystat_tmg")])
        E("vector", lambda e: e.tensor_scalar(
            ystat[:, 78:79], ystat[:, 77:78], 1.0 / 32767.0, 1e-12, ALU.mult, ALU.max),
            reads=["ystat_tmg"], writes=["ystat_st"])
        E("vector", lambda e: e.reciprocal(ystat[:, 79:80], ystat[:, 78:79]),
          reads=["ystat_st"], writes=["ystat_sti"])

        E("vector", lambda e: e.memset(smcol("smx"), 0.0), writes=["sm_smx"])
        def b3_ops(oc):
            ops = []

            def E2(eng, fn, **kw):
                ops.append(lambda e_=eng, f_=fn, k_=kw: E(e_, f_, **k_))
            bt3 = stg[oc % 2] if True else None
            bu3 = sg2p[oc % 2]
            R3t = ALLS if oc % 2 == 0 else ALLS1
            R3u = ALLS2p[oc % 2]
            E2("gpsimd", lambda e, occ=oc, b=bt3: e.dma_start(out=b[:, 0:BW], in_=d["tt_h"][occ]),
              reads=["tt_h"], writes=R3t, dma=True)
            E2("gpsimd", lambda e, occ=oc, b=bu3: e.dma_start(out=b[:, 0:BW], in_=d["rup_h"][occ]),
              reads=["rup_h"], writes=R3u, dma=True)
            E2("scalar", lambda e: e.activation(
                bt3[:, 0:BW], bt3[:, 0:BW], ACTF.Identity, bias=MAGIC, scale=ystat[:, 79:80]),
                reads=R3t + ["ystat_sti", "cst"], writes=R3t)
            E2("scalar", lambda e: e.activation(
                bt3[:, 0:BW], bt3[:, 0:BW], ACTF.Identity, bias=-MAGIC),
                reads=R3t + ["cst"], writes=R3t)
            E2("vector", lambda e: e.tensor_scalar(
                bt3[:, 0:BW], bt3[:, 0:BW], -32768.0, 32767.0, ALU.max, ALU.min),
                reads=R3t, writes=R3t)
            E2("scalar", lambda e: e.activation(
                bt3[:, 0:BW], bt3[:, 0:BW], ACTF.Identity, scale=ystat[:, 78:79]),
                reads=R3t + ["ystat_st"], writes=R3t)
            E2("scalar", lambda e: e.activation(
                bu3[:, 0:BW], bu3[:, 0:BW], ACTF.Identity, bias=MAGIC, scale=ystat[:, 73:74]),
                reads=R3u + ["ystat_sui", "cst"], writes=R3u)
            E2("scalar", lambda e: e.activation(
                bu3[:, 0:BW], bu3[:, 0:BW], ACTF.Identity, bias=-MAGIC),
                reads=R3u + ["cst"], writes=R3u)
            E2("vector", lambda e: e.tensor_scalar(
                bu3[:, 0:BW], bu3[:, 0:BW], -32768.0, 32767.0, ALU.max, ALU.min),
                reads=R3u, writes=R3u)
            E2("scalar", lambda e: e.activation(
                bu3[:, 0:BW], bu3[:, 0:BW], ACTF.Identity, scale=ystat[:, 72:73]),
                reads=R3u + ["ystat_su"], writes=R3u)
            E2("vector", lambda e, b=bt3, b2_=bu3: e.tensor_tensor(b[:, 0:BW], b[:, 0:BW], b2_[:, 0:BW], ALU.mult),
              reads=R3t + R3u, writes=R3t)
            E2("vector", lambda e, b=bt3: e.tensor_reduce(
                smcol("um3", 1), b[:, 0:BW], AX, ALU.max, apply_absolute_value=True),
                reads=R3t, writes=["sm_um3"])
            E2("vector", lambda e: e.tensor_tensor(
                smcol("smx"), smcol("smx"), smcol("um3", 1), ALU.max),
                reads=["sm_um3", "sm_smx"], writes=["sm_smx"])
            E2("gpsimd", lambda e, occ=oc, b=bt3: e.dma_start(out=d["ss_h"][occ], in_=b[:, 0:BW]),
              reads=R3t, writes=["ss_h"], dma=True)

            return ops

        for k3 in range(12):
            for a_, b_ in _it.zip_longest(b3_ops(2 * k3), b3_ops(2 * k3 + 1)):
                if a_ is not None:
                    a_()
                if b_ is not None:
                    b_()

        allreduce_scalar(3, [(smcol("smx"), "sm_smx")], [(smcol("smg"), "sm_smg")])
        E("vector", lambda e: e.tensor_scalar(
            smcol("ssc"), smcol("smg"), 1.0 / 32767.0, 1e-12, ALU.mult, ALU.max),
            reads=["sm_smg"], writes=["sm_ssc"])
        E("vector", lambda e: e.reciprocal(smcol("ssi"), smcol("ssc")),
          reads=["sm_ssc"], writes=["sm_ssi"])

        for tt in range(NTT):
            for oc in range(24):
                sl4 = (oc % 8) // 2
                bt = stg[oc % 2][:, sl4 * 512:(sl4 + 1) * 512]
                btres = [f"stg{oc % 2}_{sl4}"]
                E("gpsimd", lambda e, occ=oc, ttt=tt, b=bt: e.dma_start(
                    out=b, in_=d["ss_h"][occ, :, ttt * 512:(ttt + 1) * 512]),
                    reads=["ss_h"], writes=btres, dma=True)
                E("scalar", lambda e, b=bt: e.activation(
                    b, b, ACTF.Identity, bias=MAGIC,
                    scale=smcol("ssi")),
                    reads=btres + ["sm_ssi", "cst"], writes=btres)
                E("scalar", lambda e, b=bt: e.activation(
                    b, b, ACTF.Identity, bias=-MAGIC),
                    reads=btres + ["cst"], writes=btres)
                E("vector", lambda e, b=bt: e.tensor_scalar(
                    b, b, -32768.0, 32767.0, ALU.max, ALU.min),
                    reads=btres, writes=btres)
                E("scalar", lambda e, occ=oc, b=bt: e.activation(
                    rb[:, A2_GT + occ * 512:A2_GT + (occ + 1) * 512], b,
                    ACTF.Identity, scale=smcol("ssc")),
                    reads=btres + ["sm_ssc"],
                    writes=rcols(A2_GT + oc * 512, A2_GT + (oc + 1) * 512))
            for j in range(24):
                gi = tt * 24 + j
                if tt == 0 and j < 4:
                    sl0 = gi % 8
                    E("gpsimd", lambda e, jj=j, b=wvb[:, sl0 * 1024:(sl0 + 1) * 1024]:
                      e.dma_start(out=b, in_=d["dnT_h"][jj]),
                        reads=["dnT_h"], writes=[f"dns{sl0}"], dma=True)
                if gi + 4 < NTT * 24:
                    jn, sn = (gi + 4) % 24, (gi + 4) % 8
                    E("gpsimd", lambda e, jj=jn, b=wvb[:, sn * 1024:(sn + 1) * 1024]:
                      e.dma_start(out=b, in_=d["dnT_h"][jj]),
                        reads=["dnT_h"], writes=[f"dns{sn}"], dma=True)
                sl = gi % 8
                buf = wvb[:, sl * 1024:(sl + 1) * 1024]
                jres = rcols(A2_GT + j * 512, A2_GT + (j + 1) * 512)
                for t4 in range(4):
                    for o in range(0, 1024, 512):
                        E("tensor", lambda e, jj=j, _t=t4, o0=o, b=buf: e.matmul(
                            p[_t][:, o0:o0 + 512],
                            rb[:, A2_GT + jj * 512 + _t * 128:A2_GT + jj * 512 + (_t + 1) * 128],
                            b[:, o0:o0 + 512], start=(jj == 0), stop=(jj == 23)),
                            reads=[f"dns{sl}"] + jres,
                            writes=([("pA", "pB", "pC")[t4]] if t4 < 3 else pDres(o, o + 512)))
            for t4 in range(4):
                c = tt * 4 + t4
                pres = [("pA", "pB", "pC")[t4]] if t4 < 3 else ["pD_b0", "pD_b1"]
                xb, xbres = x0t[t4 % 2], f"x0t{t4 % 2}"
                E("gpsimd", lambda e, cc=c, b=xb: e.dma_start(
                    out=b[:], in_=d["x1_h"][cc * 128:(cc + 1) * 128, :]),
                    reads=["x1_h"], writes=[xbres], dma=True)
                E("vector", lambda e, _t=t4, b=xb: e.scalar_tensor_tensor(
                    scr[:], p[_t][:], 1.0, b[:], ALU.mult, ALU.add),
                    reads=pres + [xbres, "scr"], writes=["scr"])
                E("sync", lambda e, cc=c: e.dma_start(
                    out=d["out"][cc * 128:(cc + 1) * 128, :], in_=scr[:]),
                    reads=["scr"], writes=["out_h"], dma=True)

        for nm2, sem2 in sch.dmapool["sync"]:
            v2 = sch.count[nm2]
            if v2 > 0:
                sch.stream["sync"].append(lambda e, s=sem2, v=v2: e.wait_ge(s, v))

        sch.finalize(block)
    return nc


def kernel(**inputs):
    cons = _consts_np()
    common = {}
    for k in ("Wq", "Wk", "Wv", "Wa", "Wb", "Wo", "W1", "b1", "W2", "b2"):
        common[k] = np.ascontiguousarray(np.asarray(inputs["L0_attn_" + k], np.float32))
    common["ln1w"] = np.ascontiguousarray(np.asarray(inputs["L0_attn_ln1_w"], np.float32))
    common["ln1b"] = np.ascontiguousarray(np.asarray(inputs["L0_attn_ln1_b"], np.float32))
    common["ln2w"] = np.ascontiguousarray(np.asarray(inputs["L0_attn_ln2_w"], np.float32))
    common["ln2b"] = np.ascontiguousarray(np.asarray(inputs["L0_attn_ln2_b"], np.float32))
    common["wup"] = np.ascontiguousarray(np.asarray(inputs["L0_ffn_w_up"], np.float32))
    common["wgt"] = np.ascontiguousarray(np.asarray(inputs["L0_ffn_w_gate"], np.float32))
    common["wdn"] = np.ascontiguousarray(np.asarray(inputs["L0_ffn_w_down"], np.float32))
    common.update(cons)
    x = np.asarray(inputs["L0_attn_x"], np.float32)
    in_maps = []
    for c in range(8):
        m = dict(common)
        m["x0"] = np.ascontiguousarray(x[c])
        in_maps.append(m)
    nc = build_program()
    res = run_bass_kernel_spmd(nc, in_maps, core_ids=list(range(8)))
    return np.stack([res.results[c]["out"] for c in range(8)], axis=0)


# version 26 (final)
# speedup vs baseline: 1.0398x; 1.0398x over previous
import numpy as np
import ml_dtypes
from contextlib import ExitStack

import concourse.bass as bass
import concourse.mybir as mybir
from concourse.bass_utils import run_bass_kernel_spmd

F32 = mybir.dt.float32
BF16 = mybir.dt.bfloat16
ALU = mybir.AluOpType
ACTF = mybir.ActivationFunctionType
AX = mybir.AxisListType.X

MAGIC = 12582912.0
T, D, H = 4096, 1024, 8
F1, FQ = 4096, 3072
EPS = 1e-5
NEWTON_ITERS = 3
RACE_DETECT = False
DEBUG_DUMP = False

ENGS = ("sync", "vector", "scalar", "tensor", "gpsimd")


class Sched:
    def __init__(self, nc, ctx):
        self.nc = nc
        self.ctx = ctx
        self.stream = {e: [] for e in ENGS}
        self.phase = -1
        self.sems = {}
        self.count = {}
        self.lastwait = {e: {} for e in ENGS}
        self.resw = {}
        self.resr = {}
        self.new_phase()

    DMA_POOL = {"sync": 16, "gpsimd": 10, "scalar": 8, "vector": 2, "tensor": 2}

    def new_phase(self):
        self.phase += 1
        self.dmapool = {}
        self.dmaidx = {}
        for e in ENGS:
            name = f"sem_{e}_{self.phase}"
            sem = self.ctx.enter_context(self.nc.semaphore(name))
            self.sems[e] = (name, sem)
            self.count[name] = 0
            pool = []
            for i in range(self.DMA_POOL[e]):
                dn = f"dma_{e}_{self.phase}_{i}"
                ds = self.ctx.enter_context(self.nc.semaphore(dn))
                pool.append((dn, ds))
                self.count[dn] = 0
            self.dmapool[e] = pool
            self.dmaidx[e] = 0

    def emit(self, eng, build, reads=(), writes=(), dma=False):
        if dma:
            pool = self.dmapool[eng]
            i = self.dmaidx[eng] % len(pool)
            self.dmaidx[eng] += 1
            name, sem = pool[i]
        else:
            name, sem = self.sems[eng]
        waits = {}

        def addwait(h):
            if h is None:
                return
            heng, hname, hsem, hval, hdma, hcarry = h
            if heng == "tensor" and eng == "tensor" and not hdma and not dma:
                return
            cur = waits.get(hname)
            if cur is None or hval > cur[1]:
                waits[hname] = (hsem, hval)
            for cname, (csem, cval) in hcarry:
                cur = waits.get(cname)
                if cur is None or cval > cur[1]:
                    waits[cname] = (csem, cval)

        for r in reads:
            addwait(self.resw.get(r))
        for r in writes:
            addwait(self.resw.get(r))
            for h in self.resr.get(r, ()):
                addwait(h)
        if dma and self.count[name] > 0:
            waits[name] = (sem, self.count[name])

        lw = self.lastwait[eng]
        fw = []
        for hname, (hsem, hval) in waits.items():
            if not dma and lw.get(hname, -1) >= hval:
                continue
            if lw.get(hname, -1) < hval:
                lw[hname] = hval
            fw.append((hsem, hval))

        inc = 16 if dma else 1
        self.count[name] += inc
        carry = ()
        myh = (eng, name, sem, self.count[name], dma, carry)

        def closure(e, _w=tuple(fw), _b=build, _s=sem, _i=inc):
            for ws, wv in _w:
                e.wait_ge(ws, wv)
            _b(e).then_inc(_s, _i)

        self.stream[eng].append(closure)
        for r in reads:
            self.resr.setdefault(r, []).append(myh)
        for r in writes:
            self.resw[r] = myh
            self.resr[r] = []
        return myh

    def barrier_all(self):
        targets = []
        for e2 in ENGS:
            nm, sem = self.sems[e2]
            targets.append((nm, sem, self.count[nm]))
            for nm2, sem2 in self.dmapool[e2]:
                targets.append((nm2, sem2, self.count[nm2]))
        for e in ENGS:
            lw = self.lastwait[e]
            for nm, sem, val in targets:
                if val > 0 and lw.get(nm, -1) < val:
                    lw[nm] = val
                    self.stream[e].append(lambda en, s=sem, v=val: en.wait_ge(s, v))

    def finalize(self, block):
        for ename in ENGS:
            ops = self.stream[ename]
            getattr(block, ename)(lambda e, _ops=ops: [f(e) for f in _ops])


def _consts_np():
    i = np.arange(128)
    ident = np.eye(128, dtype=np.float32)
    return {
        "c_idf": ident,
        "c_n2i": (-2.0 * ident).astype(ml_dtypes.bfloat16),
        "c_2i": (2.0 * ident).astype(ml_dtypes.bfloat16),
        "c_idb": ident.astype(ml_dtypes.bfloat16),
        "c_tri": (i[:, None] <= i[None, :]).astype(np.float32),
        "c_sgt": (i[:, None] > i[None, :]).astype(np.float32),
        "c_lmts": np.where(i[None, :] < i[:, None], 0.0, -3e38).astype(np.float32),
        "c_lmst": np.where(i[:, None] < i[None, :], 0.0, -3e38).astype(np.float32),
    }


def build_program(T=T):
    NCH = T // 128
    NTT = T // 512
    BW = min(T, 4096)
    nc = bass.Bass("TRN2", target_bir_lowering=False, debug=False,
                   detect_race_conditions=RACE_DETECT)
    d = {}

    def din(name, shape, dt=F32):
        d[name] = nc.dram_tensor(name, list(shape), dt, kind="ExternalInput").ap()

    def dint(name, shape, dt=F32):
        d[name] = nc.dram_tensor(name, list(shape), dt).ap()

    din("x0", (T, D))
    din("Wq", (D, D)); din("Wk", (D, D)); din("Wv", (D, D))
    din("Wa", (D, H)); din("Wb", (D, H)); din("Wo", (D, D))
    din("ln1w", (D,)); din("ln1b", (D,)); din("ln2w", (D,)); din("ln2b", (D,))
    din("W1", (D, F1)); din("b1", (F1,)); din("W2", (F1, D)); din("b2", (D,))
    din("wup", (FQ, D)); din("wgt", (FQ, D)); din("wdn", (D, FQ))
    for nme in ("c_idf", "c_tri", "c_sgt", "c_lmts", "c_lmst"):
        din(nme, (128, 128), F32)
    for nme in ("c_idb", "c_n2i", "c_2i"):
        din(nme, (128, 128), BF16)
    d["out"] = nc.dram_tensor("out", [T, D], F32, kind="ExternalOutput").ap()
    dint("xa_h", (T, D)); dint("x1_h", (T, D))
    dint("w1e_h", (8, 128, 8, 512), BF16)
    dint("w2b_h", (32, 128, 1024), BF16)
    dint("upT_h", (24, 128, 1024), BF16)
    dint("gtT_h", (24, 128, 1024), BF16)
    dint("dnT_h", (24, 128, 1024), BF16)
    dint("rup_h", (24, 128, T)); dint("rgt_h", (24, 128, T))
    if DEBUG_DUMP:
        dint("dbgU_h", (T // 128, 128, 1024)); dint("dbgO_h", (T // 128, 128, 1024))
        dint("dbgM_h", (T // 128, 128, 1024)); dint("dbgG_h", (T // 128, 128, 1024))
    dint("tt_h", (24, 128, T)); dint("ss_h", (24, 128, T))
    for k in range(4):
        dint(f"ccb{k}", (1, 2)); dint(f"ccr{k}", (1, 2))

    with ExitStack() as ctx:
        def sb(name, shape, dt):
            return ctx.enter_context(nc.sbuf_tensor(name, list(shape), dt))

        idf = sb("idf", (128, 128), F32)
        idb = sb("idb", (128, 128), BF16)
        n2i = sb("n2i", (128, 128), BF16)
        twoi = sb("twoi", (128, 128), BF16)
        tri = sb("tri", (128, 128), F32)
        sgt = sb("sgt", (128, 128), F32)
        lmts = sb("lmts", (128, 128), F32)
        lmst = sb("lmst", (128, 128), F32)
        onesf = sb("onesf", (1, 128), F32)
        oneb = sb("oneb", (1, 512), BF16)
        cst = sb("cst", (128, 6), F32)
        nc.const_aps.register(F32, MAGIC, cst[:, 0:1])
        nc.const_aps.register(F32, -1.0, cst[:, 1:2])
        nc.const_aps.register(F32, -MAGIC, cst[:, 2:3])
        nc.const_aps.register(F32, 65536.0, cst[:, 3:4])
        nc.const_aps.register(F32, EPS, cst[:, 4:5])
        nc.const_aps.register(F32, 1e-6, cst[:, 5:6])

        wqb = sb("wqb", (128, 8192), BF16)
        wkb = sb("wkb", (128, 8192), BF16)
        wvb = sb("wvb", (128, 8192), BF16)
        wob = sb("wob", (128, 8192), BF16)
        wabb = sb("wabb", (128, 128), BF16)
        brq = sb("brq", (1, 1024), BF16); brk = sb("brk", (1, 1024), BF16)
        brv = sb("brv", (1, 1024), BF16); brab = sb("brab", (1, 16), BF16)
        br1 = sb("br1", (1, 4096), BF16); br2 = sb("br2", (1, 1024), BF16)
        b1s = sb("b1s", (1, 512), F32)
        lnc = sb("lnc", (128, 32), F32)
        wcast = sb("wcast", (128, 1024), BF16)
        x0t = [sb("x0t0", (128, 1024), F32), sb("x0t1", (128, 1024), F32)]
        wld = x0t
        xpt = sb("xpt", (128, 1024), F32)
        scr = sb("scr", (128, 1024), F32)
        qf = sb("qf", (128, 1024), F32)
        kf = sb("kf", (128, 1024), F32)
        vf = sb("vf", (128, 1024), F32)
        Sst = sb("Sst", (128, 1024), F32)
        sm = sb("sm", (128, 448), F32)
        lT1 = sb("lT1", (1, 1024), F32)
        negof = sb("negof", (1, 128), F32)
        rb = sb("rb", (128, 24576), BF16)
        ystat = sb("ystat", (128, 96), F32)
        stg = [sb("stg0", (128, 4096), F32), sb("stg1", (128, 4096), F32)]
        p = [ctx.enter_context(nc.psum_tensor(f"p{i}", [128, 1024], F32))
             for i in range(4)]
        pA, pB, pC, pD = p
        pCb = pC[:].bitcast(BF16)
        PN = ("pA", "pB", "pC")

        block = ctx.enter_context(nc.Block())
        sch = Sched(nc, ctx)

        RB = {}
        for idx, nmm in enumerate(
                ["nb", "nT", "ksc", "qT", "kT", "kpT", "Gs", "GsT", "GiT",
                 "Gp", "Xp0", "S0b", "Ub", "oTb",
                 "G2T0", "G2T1", "Mbv0", "Mbv1", "BT0", "BT1",
                 "qgT0", "qgT1", "ktl0", "ktl1"]):
            RB[nmm] = idx * 1024
        RB["bvb"] = RB["nb"]
        RB["kgb"] = RB["ksc"]
        RB["qsc"] = RB["nb"]
        RB["X1"] = RB["qT"]
        RB["X0"] = RB["kT"]
        RB["Xp1"] = RB["kpT"]
        RB["T1b"] = RB["GiT"]
        RB["T1p"] = RB["GsT"]
        RB["Gn"] = RB["Gs"]
        A2_HT = 1024
        A2_GT = 5120

        def rcols(lo, hi):
            return [f"rb{k}" for k in range(lo // 1024, (hi + 1023) // 1024)]

        def rap(nmm, lo=0, hi=1024):
            base = RB[nmm]
            return rb[:, base + lo:base + hi]

        def rres(nmm, lo=0, hi=1024):
            base = RB[nmm]
            return rcols(base + lo, base + hi)

        SM = {}
        _off = [0]

        SM_OVERLAY = {"ymx2": ("bmx", 48), "ymx3": ("bs", 48)}

        def smcol(nmm, w=1):
            if nmm in SM_OVERLAY:
                base, ww = SM_OVERLAY[nmm]
                assert w == ww
                o, bw = SM[base]
                return sm[:, o:o + w]
            if nmm not in SM:
                SM[nmm] = (_off[0], w)
                _off[0] += w
                assert _off[0] <= 448, nmm
            o, ww = SM[nmm]
            assert ww == w
            return sm[:, o:o + w]

        for _pre in (("bmx", 96), ("bs", 96), ("bi", 96)):
            smcol(*_pre)

        def E(eng, fn, reads=(), writes=(), dma=False):
            return sch.emit(eng, fn, reads=reads, writes=writes, dma=dma)

        def hsl(buf, h):
            return buf[:, h * 128:(h + 1) * 128]

        def psl(pp, h):
            return pp[:, h * 128:(h + 1) * 128]

        def pDres(lo, hi):
            out = []
            if lo < 512:
                out.append("pD_b0")
            if hi > 512:
                out.append("pD_b1")
            return out

        for nmm, t_ in (("c_idf", idf), ("c_idb", idb), ("c_n2i", n2i), ("c_2i", twoi),
                        ("c_tri", tri), ("c_sgt", sgt), ("c_lmts", lmts),
                        ("c_lmst", lmst)):
            E("gpsimd", lambda e, n_=nmm, tt_=t_: e.dma_start(out=tt_[:], in_=d[n_]),
              writes=[nmm], dma=True)
        E("vector", lambda e: e.memset(onesf[:], 1.0), writes=["onesf"])
        E("vector", lambda e: e.memset(oneb[:], 1.0), writes=["oneb"])
        E("vector", lambda e: e.memset(negof[:], -1.0), writes=["negof"])
        E("vector", lambda e: e.memset(cst[:, 0:1], MAGIC), writes=["cst"])
        E("vector", lambda e: e.memset(cst[:, 1:2], -1.0), writes=["cst"])
        E("vector", lambda e: e.memset(cst[:, 2:3], -MAGIC), writes=["cst"])
        E("vector", lambda e: e.memset(cst[:, 3:4], 65536.0), writes=["cst"])
        E("vector", lambda e: e.memset(cst[:, 4:5], EPS), writes=["cst"])
        E("vector", lambda e: e.memset(cst[:, 5:6], 1e-6), writes=["cst"])
        E("vector", lambda e: e.memset(Sst[:], 0.0), writes=["Sst"])
        E("vector", lambda e: e.memset(ystat[:, 64:65], 0.0), writes=["ystat_ymx"])
        E("vector", lambda e: e.memset(ystat[:, 68:69], 0.0), writes=["ystat_um"])
        E("vector", lambda e: e.memset(ystat[:, 69:70], 0.0), writes=["ystat_gm"])
        E("vector", lambda e: e.memset(smcol("ccv", 2), 0.0), writes=["sm_ccv"])
        for j in range(8):
            for nmm, c0 in (("ln1w", 0), ("ln1b", 8), ("ln2w", 16), ("ln2b", 24)):
                E("gpsimd", lambda e, n_=nmm, jj=j, c=c0: e.dma_start(
                    out=lnc[:, c + jj:c + jj + 1],
                    in_=d[n_][jj * 128:(jj + 1) * 128].rearrange("(x o) -> x o", o=1)),
                    writes=["lnc"], dma=True)
        E("gpsimd", lambda e: e.dma_start(out=br2[:], in_=d["b2"].rearrange("(o x) -> o x", o=1)),
          writes=["br2"], dma=True)
        sch.barrier_all()

        def fold_weight(wname, dst, dstres, brow, browres, ncols):
            for j in range(8):
                buf, bres = wld[j % 2], f"x0t{j % 2}"
                E("gpsimd", lambda e, jj=j, b=buf, w=wname: e.dma_start(
                    out=b[:, 0:ncols], in_=d[w][jj * 128:(jj + 1) * 128, :]),
                    writes=[bres], dma=True)
                for o in range(0, ncols, 512):
                    oe = min(o + 512, ncols)
                    E("tensor", lambda e, jj=j, b=buf, o0=o, o1=oe: e.matmul(
                        pD[0:1, o0:o1], lnc[:, 8 + jj:9 + jj], b[:, o0:o1],
                        start=(jj == 0), stop=(jj == 7)),
                        reads=[bres, "lnc"], writes=pDres(o, oe))
                E("vector", lambda e, jj=j, b=buf, dd=dst: e.tensor_scalar(
                    dd[:, jj * ncols:(jj + 1) * ncols], b[:, 0:ncols],
                    lnc[:, jj:jj + 1], None, ALU.mult),
                    reads=[bres, "lnc"], writes=[dstres])
            E("scalar", lambda e: e.copy(brow[0:1, 0:ncols], pD[0:1, 0:ncols]),
              reads=pDres(0, ncols), writes=[browres])

        fold_weight("Wq", wqb, "wqb", brq, "brq", 1024)
        fold_weight("Wk", wkb, "wkb", brk, "brk", 1024)
        fold_weight("Wv", wvb, "wvb", brv, "brv", 1024)
        for wname, coff, boff in (("Wa", 0, 0), ("Wb", 64, 8)):
            for j in range(8):
                buf, bres = wld[j % 2], f"x0t{j % 2}"
                E("gpsimd", lambda e, jj=j, b=buf, w=wname: e.dma_start(
                    out=b[:, 0:8], in_=d[w][jj * 128:(jj + 1) * 128, :]),
                    writes=[bres], dma=True)
                E("tensor", lambda e, jj=j, b=buf, bo=boff: e.matmul(
                    pD[0:1, bo:bo + 8], lnc[:, 8 + jj:9 + jj], b[:, 0:8],
                    start=(jj == 0), stop=(jj == 7)),
                    reads=[bres, "lnc"], writes=["pD_b0"])
                E("vector", lambda e, jj=j, b=buf, co=coff: e.tensor_scalar(
                    wabb[:, co + jj * 8:co + (jj + 1) * 8], b[:, 0:8],
                    lnc[:, jj:jj + 1], None, ALU.mult),
                    reads=[bres, "lnc"], writes=["wabb"])
            E("scalar", lambda e, bo=boff: e.copy(
                brab[0:1, bo:bo + 8], pD[0:1, bo:bo + 8]),
                reads=["pD_b0"], writes=["brab"])
        for j in range(8):
            E("gpsimd", lambda e, jj=j: e.dma_start(
                out=wob[:, jj * 1024:(jj + 1) * 1024],
                in_=d["Wo"][jj * 128:(jj + 1) * 128, :]),
                writes=["wob"], dma=True)
        for j in range(32):
            E("gpsimd", lambda e, jj=j: e.dma_start(
                out=d["w2b_h"][jj], in_=d["W2"][jj * 128:(jj + 1) * 128, :]),
                writes=["w2b_h"], dma=True)

        def stats_ln(src, srcres, tag):
            mc, rc = smcol("m" + tag), smcol("r" + tag)
            E("vector", lambda e: e.reduce_sum(smcol("su" + tag), src, AX),
              reads=[srcres], writes=["sm_su" + tag])
            E("scalar", lambda e: e.activation(scr[:], src, ACTF.Square,
                                               accum_out=smcol("sq" + tag)),
              reads=[srcres], writes=["scr", "sm_sq" + tag])
            E("vector", lambda e: e.tensor_scalar(mc, smcol("su" + tag),
                                                  1.0 / D, None, ALU.mult),
              reads=["sm_su" + tag], writes=["sm_m" + tag])
            E("vector", lambda e: e.tensor_scalar(
                smcol("v" + tag), mc, mc, -1.0, ALU.mult, ALU.mult),
                reads=["sm_m" + tag], writes=["sm_v" + tag])
            E("vector", lambda e: e.scalar_tensor_tensor(
                smcol("v" + tag), smcol("sq" + tag), 1.0 / D, smcol("v" + tag),
                ALU.mult, ALU.add),
                reads=["sm_sq" + tag, "sm_v" + tag], writes=["sm_v" + tag])
            E("scalar", lambda e: e.activation(smcol("v" + tag), smcol("v" + tag),
                                               ACTF.Sqrt, bias=EPS),
              reads=["sm_v" + tag, "cst"], writes=["sm_v" + tag])
            E("vector", lambda e: e.reciprocal(rc, smcol("v" + tag)),
              reads=["sm_v" + tag], writes=["sm_r" + tag])
            return mc, rc, "sm_m" + tag, "sm_r" + tag

        xpt_bufs = [(xpt, ["xpt"]), (stg[0][:, 0:1024], ["stg0_0", "stg0_1"])]

        def stage1(c):
            par = c % 2
            xb, xbres = x0t[par], f"x0t{par}"
            xp_ap, xp_res = xpt_bufs[par]
            E("gpsimd", lambda e, cc=c, b=xb: e.dma_start(
                out=b[:], in_=d["x0"][cc * 128:(cc + 1) * 128, :]),
                writes=[xbres], dma=True)
            m0, r0, m0r, r0r = stats_ln(xb[:], xbres, "0")
            E("vector", lambda e, b=xb, xp=xp_ap: e.tensor_scalar(
                xp[:], b[:], m0, r0, ALU.subtract, ALU.mult),
                reads=[xbres, m0r, r0r] + xp_res, writes=xp_res)
            m1, r1, m1r, r1r = stats_ln(xp_ap[:], xp_res[0] if False else "xp1_" + str(par), "1") \
                if False else stats_ln(xp_ap[:], xp_res[0], "1")
            E("vector", lambda e, xp=xp_ap: e.tensor_scalar(
                rap("nb"), xp[:], m1, r1, ALU.subtract, ALU.mult),
                reads=xp_res + [m1r, r1r], writes=rres("nb"))
            for j in range(8):
                E("tensor", lambda e, jj=j: e.transpose(
                    pCb[:, jj * 128:(jj + 1) * 128],
                    rap("nb", jj * 128, (jj + 1) * 128), idb[:]),
                    reads=rres("nb") + ["c_idb"], writes=["pC"])
            E("scalar", lambda e: e.copy(rap("nT"), pCb[:, 0:1024]),
              reads=["pC"] + rres("nT"), writes=rres("nT"))
            for pi, (wsb, wres, brow, browres) in enumerate(
                    ((wqb, "wqb", brq, "brq"), (wkb, "wkb", brk, "brk"),
                     (wvb, "wvb", brv, "brv"))):
                pp, pres = p[pi], PN[pi]
                for o in range(0, 1024, 512):
                    for j in range(8):
                        E("tensor", lambda e, jj=j, o0=o, _p=pp, _w=wsb: e.matmul(
                            _p[:, o0:o0 + 512], rap("nT", jj * 128, (jj + 1) * 128),
                            _w[:, jj * 1024 + o0:jj * 1024 + o0 + 512],
                            start=(jj == 0), stop=False),
                            reads=rres("nT") + [wres], writes=[pres])
                    E("tensor", lambda e, o0=o, _p=pp, _b=brow: e.matmul(
                        _p[:, o0:o0 + 512], oneb[0:1, 0:128], _b[0:1, o0:o0 + 512],
                        start=False, stop=True),
                        reads=["oneb", browres], writes=[pres])
            E("scalar", lambda e: e.activation(qf[:], pA[:], ACTF.Silu),
              reads=["pA"], writes=["qf"])
            E("scalar", lambda e: e.activation(kf[:], pB[:], ACTF.Silu),
              reads=["pB"], writes=["kf"])
            E("scalar", lambda e: e.activation(vf[:], pC[:], ACTF.Silu),
              reads=["pC"], writes=["vf"])
            for boff in (0, 8):
                for j in range(8):
                    E("tensor", lambda e, jj=j, bo=boff: e.matmul(
                        pA[:, bo:bo + 8], rap("nT", jj * 128, (jj + 1) * 128),
                        wabb[:, bo * 8 + jj * 8:bo * 8 + (jj + 1) * 8],
                        start=(jj == 0), stop=False),
                        reads=rres("nT") + ["wabb"], writes=["pA"])
                E("tensor", lambda e, bo=boff: e.matmul(
                    pA[:, bo:bo + 8], oneb[0:1, 0:128], brab[0:1, bo:bo + 8],
                    start=False, stop=True),
                    reads=["oneb", "brab"], writes=["pA"])
            E("scalar", lambda e: e.activation(smcol("ab", 16), pA[:, 0:16], ACTF.Sigmoid),
              reads=["pA"], writes=["sm_ab"])
            E("scalar", lambda e: e.activation(smcol("la", 8), smcol("ab", 16)[:, 0:8], ACTF.Ln),
              reads=["sm_ab"], writes=["sm_la"])
            for h in range(8):
                E("tensor", lambda e, hh=h: e.matmul(
                    pB[0:1, hh * 128:(hh + 1) * 128], smcol("la", 8)[:, hh:hh + 1],
                    tri[:], start=True, stop=True),
                    reads=["sm_la", "c_tri"], writes=["pB"])
            E("vector", lambda e: e.tensor_copy(lT1[:], pB[0:1, 0:1024]),
              reads=["pB"], writes=["lT1"])
            E("tensor", lambda e: e.matmul(pC[:, 0:8], tri[:], smcol("la", 8),
                                           start=True, stop=True),
              reads=["sm_la", "c_tri"], writes=["pC"])
            E("tensor", lambda e: e.matmul(pC[:, 8:16], sgt[:], smcol("la", 8),
                                           start=True, stop=True),
              reads=["sm_la", "c_sgt"], writes=["pC"])
            E("scalar", lambda e, cc=c: e.activation(smcol(f"gC{c % 2}", 8), pC[:, 0:8], ACTF.Exp)
              if False else e.activation(smcol("gam", 8), pC[:, 0:8], ACTF.Exp),
              reads=["pC"], writes=["sm_gam"])
            E("scalar", lambda e: e.activation(smcol("wde", 8), pC[:, 8:16], ACTF.Exp),
              reads=["pC"], writes=["sm_wde"])
            E("vector", lambda e, pr=par: e.tensor_tensor(
                smcol("gC0" if pr == 0 else "gC1", 8), smcol("gam", 8),
                smcol("wde", 8), ALU.mult),
                reads=["sm_gam", "sm_wde"], writes=[f"sm_gC{par}"])
            for (src, sres, nmm) in ((qf, "qf", "rq"), (kf, "kf", "rk")):
                E("scalar", lambda e, s=src: e.activation(scr[:], s[:], ACTF.Square),
                  reads=[sres, "scr"], writes=["scr"])
                E("vector", lambda e, n_=nmm: e.tensor_reduce(
                    smcol(n_, 8), scr[:].rearrange("p (h k) -> p h k", k=128),
                    AX, ALU.add),
                    reads=["scr"], writes=["sm_" + nmm])
                E("scalar", lambda e, n_=nmm: e.activation(
                    smcol(n_, 8), smcol(n_, 8), ACTF.Sqrt, bias=1e-6),
                    reads=["sm_" + nmm, "cst"], writes=["sm_" + nmm])
                E("vector", lambda e, n_=nmm: e.reciprocal(smcol(n_, 8), smcol(n_, 8)),
                  reads=["sm_" + nmm], writes=["sm_" + nmm])
            gcres = f"sm_gC{par}"
            gccol = smcol("gC0" if par == 0 else "gC1", 8)
            E("vector", lambda e, g=gccol: e.tensor_tensor(
                smcol("rgam", 8), smcol("rq", 8), smcol("gam", 8), ALU.mult),
                reads=["sm_rq", "sm_gam"], writes=["sm_rgam"])
            E("vector", lambda e: e.scalar_tensor_tensor(
                smcol("rkb", 8), smcol("rk", 8), -1.0, smcol("ab", 16)[:, 8:16],
                ALU.mult, ALU.mult),
                reads=["sm_rk", "sm_ab"], writes=["sm_rkb"])
            E("vector", lambda e: e.tensor_tensor(smcol("rkw", 8), smcol("rk", 8),
                                                  smcol("wde", 8), ALU.mult),
              reads=["sm_rk", "sm_wde"], writes=["sm_rkw"])
            E("vector", lambda e: e.scalar_tensor_tensor(
                smcol("bgn", 8), smcol("ab", 16)[:, 8:16], -1.0, smcol("gam", 8),
                ALU.mult, ALU.mult),
                reads=["sm_ab", "sm_gam"], writes=["sm_bgn"])
            E("vector", lambda e: e.tensor_tensor(smcol("rkbg", 8), smcol("rk", 8),
                                                  smcol("bgn", 8), ALU.mult),
              reads=["sm_rk", "sm_bgn"], writes=["sm_rkbg"])

            def cast8(dstname, src, sres, colap, colres):
                E("vector", lambda e: e.tensor_tensor(
                    rap(dstname).rearrange("p (h f) -> p h f", f=128),
                    src[:].rearrange("p (h f) -> p h f", f=128),
                    colap.rearrange("p (a b) -> p a b", b=1).broadcast_to([128, 8, 128]),
                    ALU.mult),
                    reads=[sres, colres] + rres(dstname), writes=rres(dstname))

            def dmaT8(dstname, srcname):
                for h in range(8):
                    E("tensor", lambda e, hh=h, sn=srcname: e.transpose(
                        pCb[:, hh * 128:(hh + 1) * 128],
                        hsl(rap(sn), hh), idb[:]),
                        reads=rres(srcname) + ["c_idb"], writes=["pC"])
                E("scalar", lambda e, dn=dstname: e.copy(rap(dn), pCb[:, 0:1024]),
                  reads=["pC"] + rres(dstname), writes=rres(dstname))

            qgT = f"qgT{par}"
            ktl = f"ktl{par}"
            cast8("qsc", qf, "qf", smcol("rq", 8), "sm_rq")
            dmaT8("qT", "qsc")
            cast8("qsc", qf, "qf", smcol("rgam", 8), "sm_rgam")
            dmaT8(qgT, "qsc")
            cast8("qsc", kf, "kf", smcol("rk", 8), "sm_rk")
            dmaT8("kT", "qsc")
            cast8("ksc", kf, "kf", smcol("rkb", 8), "sm_rkb")
            dmaT8("kpT", "ksc")
            cast8(ktl, kf, "kf", smcol("rkw", 8), "sm_rkw")
            for h in range(8):
                E("tensor", lambda e, hh=h: e.matmul(
                    psl(pA, hh), lT1[0:1, hh * 128:(hh + 1) * 128], onesf[:],
                    start=True, stop=False),
                    reads=["lT1", "onesf"], writes=["pA"])
                E("tensor", lambda e, hh=h: e.matmul(
                    psl(pA, hh), negof[:], lT1[0:1, hh * 128:(hh + 1) * 128],
                    start=False, stop=True),
                    reads=["lT1", "negof"], writes=["pA"])
                E("tensor", lambda e, hh=h: e.matmul(
                    psl(pB, hh), onesf[:], lT1[0:1, hh * 128:(hh + 1) * 128],
                    start=True, stop=False),
                    reads=["lT1", "onesf"], writes=["pB"])
                E("tensor", lambda e, hh=h: e.matmul(
                    psl(pB, hh), lT1[0:1, hh * 128:(hh + 1) * 128], negof[:],
                    start=False, stop=True),
                    reads=["lT1", "negof"], writes=["pB"])
            lm_b = lmts[:].rearrange("p (o f) -> p o f", o=1).broadcast_to([128, 8, 128])
            lm_b2 = lmst[:].rearrange("p (o f) -> p o f", o=1).broadcast_to([128, 8, 128])
            id_b8 = idb[:].rearrange("p (o f) -> p o f", o=1).broadcast_to([128, 8, 128])
            E("vector", lambda e: e.tensor_tensor(
                scr[:].rearrange("p (h f) -> p h f", f=128),
                pA[:].rearrange("p (h f) -> p h f", f=128), lm_b, ALU.add),
                reads=["pA", "c_lmts", "scr"], writes=["scr"])
            E("scalar", lambda e: e.activation(rap("Gs"), scr[:], ACTF.Exp),
              reads=["scr"], writes=rres("Gs"))
            E("vector", lambda e: e.tensor_tensor(
                scr[:].rearrange("p (h f) -> p h f", f=128),
                pB[:].rearrange("p (h f) -> p h f", f=128), lm_b2, ALU.add),
                reads=["pB", "c_lmst", "scr"], writes=["scr"])
            E("scalar", lambda e: e.activation(rap("GsT"), scr[:], ACTF.Exp),
              reads=["scr"], writes=rres("GsT"))
            E("vector", lambda e: e.tensor_tensor(
                rap("GiT").rearrange("p (h f) -> p h f", f=128),
                rap("GsT").rearrange("p (h f) -> p h f", f=128), id_b8, ALU.add),
                reads=rres("GsT") + ["c_idb"], writes=rres("GiT"))
            for h in range(8):
                E("tensor", lambda e, hh=h: e.matmul(
                    psl(pA, hh), hsl(rap("kpT"), hh), hsl(rap("kT"), hh),
                    start=True, stop=True),
                    reads=rres("kpT") + rres("kT"), writes=["pA"])
                E("tensor", lambda e, hh=h: e.matmul(
                    psl(pB, hh), hsl(rap("kT"), hh), hsl(rap("kpT"), hh),
                    start=True, stop=True),
                    reads=rres("kpT") + rres("kT"), writes=["pB"])
                E("tensor", lambda e, hh=h: e.matmul(
                    psl(pC, hh), hsl(rap("kT"), hh), hsl(rap("qT"), hh),
                    start=True, stop=True),
                    reads=rres("kT") + rres("qT"), writes=["pC"])
            bt = f"BT{par}"
            E("vector", lambda e, b=bt: e.tensor_tensor(rap(b), pC[:], rap("GiT"), ALU.mult),
              reads=["pC"] + rres("GiT") + rres(bt), writes=rres(bt))
            E("vector", lambda e: e.tensor_tensor(rap("X0"), pA[:], rap("Gs"), ALU.mult),
              reads=["pA"] + rres("Gs") + rres("X0"), writes=rres("X0"))
            E("vector", lambda e: e.tensor_tensor(
                rap("X0").rearrange("p (h f) -> p h f", f=128),
                rap("X0").rearrange("p (h f) -> p h f", f=128), id_b8, ALU.add),
                reads=rres("X0") + ["c_idb"], writes=rres("X0"))
            E("vector", lambda e: e.tensor_tensor(rap("T1b"), pB[:], rap("GsT"), ALU.mult),
              reads=["pB"] + rres("GsT") + rres("T1b"), writes=rres("T1b"))
            E("vector", lambda e: e.tensor_tensor(
                rap("Xp0").rearrange("p (h f) -> p h f", f=128),
                rap("T1b").rearrange("p (h f) -> p h f", f=128), id_b8, ALU.add),
                reads=rres("T1b") + ["c_idb"] + rres("Xp0"), writes=rres("Xp0"))
            E("vector", lambda e: e.scalar_tensor_tensor(
                rap("Gp").rearrange("p (h f) -> p h f", f=128),
                rap("T1b").rearrange("p (h f) -> p h f", f=128), -1.0, id_b8,
                ALU.mult, ALU.add),
                reads=rres("T1b") + ["c_idb"] + rres("Gp"), writes=rres("Gp"))
            E("vector", lambda e: e.scalar_tensor_tensor(
                rap("Gn").rearrange("p (h f) -> p h f", f=128),
                rap("X0").rearrange("p (h f) -> p h f", f=128), -1.0,
                twoi[:].rearrange("p (o f) -> p o f", o=1).broadcast_to([128, 8, 128]),
                ALU.mult, ALU.add),
                reads=rres("X0") + ["c_2i"] + rres("Gn"), writes=rres("Gn"))
            xcur, xpcur = "X0", "Xp0"
            for it in range(NEWTON_ITERS):
                for h in range(8):
                    E("tensor", lambda e, hh=h, xc=xcur: e.matmul(
                        psl(pA, hh), hsl(rap("Gp"), hh), hsl(rap(xc), hh),
                        start=True, stop=True),
                        reads=rres("Gp") + rres(xcur), writes=["pA"])
                    E("tensor", lambda e, hh=h, xpc=xpcur: e.matmul(
                        psl(pB, hh), hsl(rap("Gn"), hh), hsl(rap(xpc), hh),
                        start=True, stop=True),
                        reads=rres("Gn") + rres(xpcur), writes=["pB"])
                E("scalar", lambda e: e.copy(rap("T1b"), pA[:]),
                  reads=["pA"] + rres("T1b"), writes=rres("T1b"))
                E("vector", lambda e: e.tensor_copy(rap("T1p"), pB[:]),
                  reads=["pB"] + rres("T1p"), writes=rres("T1p"))
                xnext = "X1" if xcur == "X0" else "X0"
                xpnext = "Xp1" if xpcur == "Xp0" else "Xp0"
                for h in range(8):
                    E("tensor", lambda e, hh=h, xpc=xpcur: e.matmul(
                        psl(pA, hh), hsl(rap(xpc), hh), hsl(rap("T1b"), hh),
                        start=True, stop=False),
                        reads=rres(xpcur) + rres("T1b"), writes=["pA"])
                    E("tensor", lambda e, hh=h, xpc=xpcur: e.matmul(
                        psl(pA, hh), hsl(rap(xpc), hh), n2i[:],
                        start=False, stop=True),
                        reads=rres(xpcur) + ["c_n2i"], writes=["pA"])
                    E("tensor", lambda e, hh=h, xc=xcur: e.matmul(
                        psl(pB, hh), hsl(rap(xc), hh), hsl(rap("T1p"), hh),
                        start=True, stop=False),
                        reads=rres(xcur) + rres("T1p"), writes=["pB"])
                    E("tensor", lambda e, hh=h, xc=xcur: e.matmul(
                        psl(pB, hh), hsl(rap(xc), hh), n2i[:],
                        start=False, stop=True),
                        reads=rres(xcur) + ["c_n2i"], writes=["pB"])
                E("scalar", lambda e, xn=xnext: e.mul(rap(xn), pA[:], -1.0),
                  reads=["pA"] + rres(xnext), writes=rres(xnext))
                E("vector", lambda e, xpn=xpnext: e.tensor_scalar(
                    rap(xpn), pB[:], -1.0, None, ALU.mult),
                    reads=["pB"] + rres(xpnext), writes=rres(xpnext))
                xcur, xpcur = xnext, xpnext
            mbv = f"Mbv{par}"
            g2t = f"G2T{par}"
            cast8("bvb", vf, "vf", smcol("ab", 16)[:, 8:16], "sm_ab")
            cast8("kgb", kf, "kf", smcol("rkbg", 8), "sm_rkbg")
            for h in range(8):
                E("tensor", lambda e, hh=h, xpc=xpcur: e.matmul(
                    psl(pC, hh), hsl(rap(xpc), hh), hsl(rap("bvb"), hh),
                    start=True, stop=True),
                    reads=rres(xpcur) + rres("bvb"), writes=["pC"])
            E("scalar", lambda e, m=mbv: e.copy(rap(m), pC[:]),
              reads=["pC"] + rres(mbv), writes=rres(mbv))
            for h in range(8):
                E("tensor", lambda e, hh=h, xpc=xpcur: e.matmul(
                    psl(pC, hh), hsl(rap("kgb"), hh), hsl(rap(xpc), hh),
                    start=True, stop=True),
                    reads=rres(xpcur) + rres("kgb"), writes=["pC"])
            E("scalar", lambda e, g=g2t: e.copy(rap(g), pC[:]),
              reads=["pC"] + rres(g2t), writes=rres(g2t))

        def stage2(c):
            par = c % 2
            qgT, ktl = f"qgT{par}", f"ktl{par}"
            mbv, g2t, bt = f"Mbv{par}", f"G2T{par}", f"BT{par}"
            gcres = f"sm_gC{par}"
            gccol = smcol("gC0" if par == 0 else "gC1", 8)
            xp_ap, xp_res = xpt_bufs[par]
            E("vector", lambda e: e.tensor_copy(rap("S0b"), Sst[:]),
              reads=["Sst"] + rres("S0b"), writes=rres("S0b"))
            for h in range(8):
                E("tensor", lambda e, hh=h, g=g2t: e.matmul(
                    psl(pD, hh), hsl(rap(g), hh), hsl(rap("S0b"), hh),
                    start=True, stop=False),
                    reads=rres(g2t) + rres("S0b"), writes=["pD_b0", "pD_b1"])
                E("tensor", lambda e, hh=h, m=mbv: e.matmul(
                    psl(pD, hh), idb[:], hsl(rap(m), hh),
                    start=False, stop=True),
                    reads=["c_idb"] + rres(mbv), writes=["pD_b0", "pD_b1"])
            E("scalar", lambda e: e.copy(rap("Ub"), pD[:]),
              reads=["pD_b0", "pD_b1"] + rres("Ub"), writes=rres("Ub"))
            for h in range(8):
                E("tensor", lambda e, hh=h, q=qgT: e.matmul(
                    psl(pD, hh), hsl(rap("S0b"), hh), hsl(rap(q), hh),
                    start=True, stop=False),
                    reads=rres("S0b") + rres(qgT), writes=["pD_b0", "pD_b1"])
                E("tensor", lambda e, hh=h, b=bt: e.matmul(
                    psl(pD, hh), hsl(rap("Ub"), hh), hsl(rap(b), hh),
                    start=False, stop=True),
                    reads=rres("Ub") + rres(bt), writes=["pD_b0", "pD_b1"])
            E("scalar", lambda e: e.copy(rap("oTb"), pD[:]),
              reads=["pD_b0", "pD_b1"] + rres("oTb"), writes=rres("oTb"))
            if DEBUG_DUMP:
                E("gpsimd", lambda e, cc=c: e.dma_start(out=d["dbgU_h"][cc], in_=rap("Ub")),
                  reads=rres("Ub"), writes=["dbgU"], dma=True)
                E("gpsimd", lambda e, cc=c: e.dma_start(out=d["dbgO_h"][cc], in_=rap("oTb")),
                  reads=rres("oTb"), writes=["dbgO"], dma=True)
                E("gpsimd", lambda e, cc=c, m=mbv: e.dma_start(out=d["dbgM_h"][cc], in_=rap(m)),
                  reads=rres(mbv), writes=["dbgM"], dma=True)
                E("gpsimd", lambda e, cc=c, g=g2t: e.dma_start(out=d["dbgG_h"][cc], in_=rap(g)),
                  reads=rres(g2t), writes=["dbgG"], dma=True)
            E("vector", lambda e, g=gccol: e.tensor_tensor(
                Sst[:].rearrange("p (h f) -> p h f", f=128),
                Sst[:].rearrange("p (h f) -> p h f", f=128),
                g.rearrange("p (a b) -> p a b", b=1).broadcast_to([128, 8, 128]),
                ALU.mult),
                reads=["Sst", gcres], writes=["Sst"])
            for h in range(8):
                E("tensor", lambda e, hh=h, k=ktl: e.matmul(
                    psl(pD, hh), hsl(rap(k), hh), hsl(rap("Ub"), hh),
                    start=True, stop=True),
                    reads=rres(ktl) + rres("Ub"), writes=["pD_b0", "pD_b1"])
            E("vector", lambda e: e.scalar_tensor_tensor(
                Sst[:], pD[:], 1.0, Sst[:], ALU.mult, ALU.add),
                reads=["pD_b0", "pD_b1", "Sst"], writes=["Sst"])
            for o in range(0, 1024, 512):
                for j in range(8):
                    E("tensor", lambda e, jj=j, o0=o: e.matmul(
                        pD[:, o0:o0 + 512], hsl(rap("oTb"), jj),
                        wob[:, jj * 1024 + o0:jj * 1024 + o0 + 512],
                        start=(jj == 0), stop=(jj == 7)),
                        reads=rres("oTb") + ["wob"], writes=pDres(o, o + 512))
            E("vector", lambda e, xp=xp_ap: e.scalar_tensor_tensor(
                xp[:], pD[:], 1.0, xp[:], ALU.mult, ALU.add),
                reads=["pD_b0", "pD_b1"] + xp_res, writes=xp_res)
            E("sync", lambda e, cc=c, xp=xp_ap: e.dma_start(
                out=d["xa_h"][cc * 128:(cc + 1) * 128, :], in_=xp[:]),
                reads=xp_res, writes=["xa_h"], dma=True)

        for c in range(NCH + 1):
            if c < NCH:
                stage1(c)
            if c >= 1:
                stage2(c - 1)

        for g in range(8):
            for j in range(8):
                buf, bres = wld[j % 2], f"x0t{j % 2}"
                E("gpsimd", lambda e, gg=g, jj=j, b=buf: e.dma_start(
                    out=b[:, 0:512],
                    in_=d["W1"][jj * 128:(jj + 1) * 128, gg * 512:(gg + 1) * 512]),
                    writes=[bres], dma=True)
                E("tensor", lambda e, jj=j, b=buf: e.matmul(
                    pD[0:1, 0:512], lnc[:, 24 + jj:25 + jj], b[:, 0:512],
                    start=(jj == 0), stop=(jj == 7)),
                    reads=[bres, "lnc"], writes=["pD_b0"])
                E("vector", lambda e, b=buf, jj=j: e.tensor_scalar(
                    wcast[:, 0:512], b[:, 0:512], lnc[:, 16 + jj:17 + jj],
                    None, ALU.mult),
                    reads=[bres, "lnc", "wcast"], writes=["wcast"])
                E("gpsimd", lambda e, gg=g, jj=j: e.dma_start(
                    out=d["w1e_h"][gg, :, jj, :], in_=wcast[:, 0:512]),
                    reads=["wcast"], writes=["w1e_h"], dma=True)
            E("gpsimd", lambda e, gg=g: e.dma_start(
                out=b1s[:], in_=d["b1"][gg * 512:(gg + 1) * 512].rearrange("(o x) -> o x", o=1)),
                writes=["b1s"], dma=True)
            E("vector", lambda e, gg=g: e.scalar_tensor_tensor(
                br1[0:1, gg * 512:(gg + 1) * 512], pD[0:1, 0:512], 1.0,
                b1s[0:1, :], ALU.mult, ALU.add),
                reads=["pD_b0", "b1s"], writes=["br1"])

        deq_lists = {}

        def dequant_emit(src, dst, nchunks, width, nblk):
            nj = width // 128
            lst = deq_lists.setdefault(dst, [])

            def blkA(pc, gi, src=src, dst=dst, width=width, nblk=nblk, nj=nj):
                buf, bres = stg[gi % 2], [f"stg{gi % 2}_{k}" for k in range(8)]
                E("gpsimd", lambda e, pcc=pc, b=buf: e.dma_start(
                    out=b[:, 0:width], in_=d[src][pcc * 128:(pcc + 1) * 128, :]),
                    writes=bres, dma=True)
                E("vector", lambda e, b=buf: e.tensor_reduce(
                    smcol("bmx", 96)[:, 0:nblk],
                    b[:, 0:width].rearrange("p (n k) -> p n k", k=32), AX,
                    ALU.max, apply_absolute_value=True),
                    reads=bres, writes=["sm_bmx"])
                E("vector", lambda e: e.tensor_scalar(
                    smcol("bs", 96)[:, 0:nblk], smcol("bmx", 96)[:, 0:nblk],
                    1.0 / 7.0, 1e-12, ALU.mult, ALU.max),
                    reads=["sm_bmx"], writes=["sm_bs"])
                E("vector", lambda e: e.reciprocal(
                    smcol("bi", 96)[:, 0:nblk], smcol("bs", 96)[:, 0:nblk]),
                    reads=["sm_bs"], writes=["sm_bi"])
                bi_b = smcol("bi", 96)[:, 0:nblk].rearrange("p (n o) -> p n o", o=1).broadcast_to([128, nblk, 32])
                bs_b = smcol("bs", 96)[:, 0:nblk].rearrange("p (n o) -> p n o", o=1).broadcast_to([128, nblk, 32])
                E("vector", lambda e, b=buf: e.tensor_tensor(
                    b[:, 0:width].rearrange("p (n k) -> p n k", k=32),
                    b[:, 0:width].rearrange("p (n k) -> p n k", k=32), bi_b, ALU.mult),
                    reads=bres + ["sm_bi"], writes=bres)
                E("vector", lambda e, b=buf: e.tensor_scalar(
                    b[:, 0:width], b[:, 0:width], MAGIC, MAGIC, ALU.add, ALU.subtract),
                    reads=bres, writes=bres)
                E("vector", lambda e, b=buf: e.tensor_scalar(
                    b[:, 0:width], b[:, 0:width], -8.0, 7.0, ALU.max, ALU.min),
                    reads=bres, writes=bres)
                E("vector", lambda e, b=buf: e.tensor_tensor(
                    b[:, 0:width].rearrange("p (n k) -> p n k", k=32),
                    b[:, 0:width].rearrange("p (n k) -> p n k", k=32), bs_b, ALU.mult),
                    reads=bres + ["sm_bs"], writes=bres)

            def blkB(pc, gi, src=src, dst=dst, width=width, nblk=nblk, nj=nj):
                buf, bres = stg[gi % 2], [f"stg{gi % 2}_{k}" for k in range(8)]
                DQALL = [f"dq_s{k}" for k in range(8)]
                for w0 in range(0, nj, 8):
                    for j in range(w0, min(w0 + 8, nj)):
                        sl = j % 8
                        E("tensor", lambda e, jj=j, s=sl, b=buf: e.transpose(
                            pC[:, s * 128:(s + 1) * 128],
                            b[:, jj * 128:(jj + 1) * 128], idf[:]),
                            reads=[bres[j // 4], "c_idf"], writes=["pC"])
                    E("scalar", lambda e: e.copy(rb[:, 0:1024], pC[:, 0:1024]),
                      reads=["pC"], writes=DQALL)
                    for j in range(w0, min(w0 + 8, nj)):
                        sl = j % 8
                        if dst == "dnT_h":
                            E("sync", lambda e, pcc=pc, jj=j, s=sl: e.dma_start(
                                out=d["dnT_h"][jj, :, pcc * 128:(pcc + 1) * 128],
                                in_=rb[:, s * 128:(s + 1) * 128]),
                                reads=[f"dq_s{sl}"], writes=["dnT_h"], dma=True)
                        else:
                            E("sync", lambda e, pcc=pc, jj=j, s=sl, dn=dst: e.dma_start(
                                out=d[dn][pcc, :, jj * 128:(jj + 1) * 128],
                                in_=rb[:, s * 128:(s + 1) * 128]),
                                reads=[f"dq_s{sl}"], writes=[dst], dma=True)

            for pc in range(nchunks):
                lst.append((pc, blkA, blkB))

        dequant_emit("wup", "upT_h", 24, 1024, 32)
        dequant_emit("wgt", "gtT_h", 24, 1024, 32)
        dequant_emit("wdn", "dnT_h", 8, 3072, 96)
        deq_pairs = []
        _dn = deq_lists["dnT_h"]
        for k in range(24):
            srcs = [deq_lists["upT_h"][k], deq_lists["gtT_h"][k]]
            if k % 3 == 2:
                srcs.append(_dn[k // 3])
            for pc, fa, fb in srcs:
                g = len(deq_pairs)
                deq_pairs.append((lambda p_=pc, f_=fa, g_=g: f_(p_, g_),
                                  lambda p_=pc, f_=fb, g_=g: f_(p_, g_)))
        deq_pos = [0, 0]

        def deq_step():
            if deq_pos[0] < len(deq_pairs):
                deq_pairs[deq_pos[0]][0]()
                deq_pos[0] += 1
            while deq_pos[1] < deq_pos[0] - 1:
                deq_pairs[deq_pos[1]][1]()
                deq_pos[1] += 1

        def deq_drain():
            while deq_pos[1] < len(deq_pairs):
                if deq_pos[0] < len(deq_pairs):
                    deq_pairs[deq_pos[0]][0]()
                    deq_pos[0] += 1
                deq_pairs[deq_pos[1]][1]()
                deq_pos[1] += 1

        sch.barrier_all()
        for j2 in range(16):
            cb = wob if j2 < 8 else wvb
            sl2 = j2 % 8
            E("gpsimd", lambda e, jj=j2, b=cb[:, sl2 * 1024:(sl2 + 1) * 1024]:
              e.dma_start(out=b, in_=d["w2b_h"][jj]),
                writes=[f"w2c{j2}"], dma=True)
        w1ld = [(wqb[:, 0:4096], ["w1s0"]), (wqb[:, 4096:8192], ["w1s1"])]
        w2ld = [(wkb[:, 0:1024], ["w2s0"]), (wkb[:, 1024:2048], ["w2s1"])]
        def a2_cs(tt):
            for cs in range(4):
                c = tt * 4 + cs
                xb, xbres = x0t[cs % 2], f"x0t{cs % 2}"
                E("gpsimd", lambda e, cc=c, b=xb: e.dma_start(
                    out=b[:], in_=d["xa_h"][cc * 128:(cc + 1) * 128, :]),
                    reads=["xa_h"], writes=[xbres], dma=True)
                m2, r2, m2r, r2r = stats_ln(xb[:], xbres, "2")
                E("vector", lambda e, b=xb: e.tensor_scalar(
                    rap("nb"), b[:], m2, r2, ALU.subtract, ALU.mult),
                    reads=[xbres, m2r, r2r], writes=rres("nb"))
                for j in range(8):
                    E("sync",
                      lambda e, jj=j, css=cs: e.dma_start_transpose(
                        rb[:, A2_HT + jj * 512 + css * 128:A2_HT + jj * 512 + (css + 1) * 128],
                        rap("nb", jj * 128, (jj + 1) * 128)),
                        reads=rres("nb"),
                        writes=rcols(A2_HT + j * 512 + cs * 128,
                                     A2_HT + j * 512 + (cs + 1) * 128), dma=True)

        def a2_w1(tt):
            for g in range(8):
                buf, bufres = w1ld[g % 2]
                for hf in range(2):
                    E("gpsimd", lambda e, gg=g, b=buf, h_=hf: e.dma_start(
                        out=b[:, h_ * 2048:(h_ + 1) * 2048].rearrange(
                            "p (j f) -> p j f", f=512),
                        in_=d["w1e_h"][gg, :, h_ * 4:(h_ + 1) * 4, :]),
                        writes=bufres, dma=True)
                for oc4 in range(4):
                    pp, ppres = (pA, "pA") if oc4 % 2 == 0 else (pB, "pB")
                    for j in range(8):
                        E("tensor", lambda e, jj=j, o4=oc4, b=buf, _p=pp: e.matmul(
                            _p[:, 0:512],
                            b[:, jj * 512 + o4 * 128:jj * 512 + (o4 + 1) * 128],
                            rb[:, A2_HT + jj * 512:A2_HT + (jj + 1) * 512],
                            start=(jj == 0), stop=False),
                            reads=bufres + rcols(A2_HT + j * 512, A2_HT + (j + 1) * 512),
                            writes=[ppres])
                    oc = g * 4 + oc4
                    E("tensor", lambda e, occ=oc, _p=pp: e.matmul(
                        _p[:, 0:512], br1[0:1, occ * 128:(occ + 1) * 128],
                        oneb[0:1, 0:512], start=False, stop=True),
                        reads=["br1", "oneb"], writes=[ppres])
                    E("scalar", lambda e, occ=oc, _p=pp: e.activation(
                        rb[:, A2_GT + occ * 512:A2_GT + (occ + 1) * 512],
                        _p[:, 0:512], ACTF.Gelu),
                        reads=[ppres],
                        writes=rcols(A2_GT + oc * 512, A2_GT + (oc + 1) * 512))
                deq_step()

        def a2_w2(tt):
            for j2 in range(16, 20):
                sl2 = j2 % 8
                E("gpsimd", lambda e, jj=j2, b=wkb[:, sl2 * 1024:(sl2 + 1) * 1024]:
                  e.dma_start(out=b, in_=d["w2b_h"][jj]),
                    writes=[f"w2s{sl2}"], dma=True)
            for j in range(32):
                if j < 8:
                    buf, bufres = wob[:, (j % 8) * 1024:(j % 8 + 1) * 1024], [f"w2c{j}"]
                elif j < 16:
                    buf, bufres = wvb[:, (j % 8) * 1024:(j % 8 + 1) * 1024], [f"w2c{j}"]
                else:
                    if j + 4 < 32:
                        jn2, sn2 = j + 4, (j + 4) % 8
                        E("gpsimd", lambda e, jj=jn2, b=wkb[:, sn2 * 1024:(sn2 + 1) * 1024]:
                          e.dma_start(out=b, in_=d["w2b_h"][jj]),
                            writes=[f"w2s{sn2}"], dma=True)
                    buf, bufres = wkb[:, (j % 8) * 1024:(j % 8 + 1) * 1024], [f"w2s{j % 8}"]
                for t4 in range(4):
                    for o in range(0, 1024, 512):
                        E("tensor", lambda e, jj=j, _t=t4, o0=o, b=buf: e.matmul(
                            p[_t][:, o0:o0 + 512],
                            rb[:, A2_GT + jj * 512 + _t * 128:A2_GT + jj * 512 + (_t + 1) * 128],
                            b[:, o0:o0 + 512], start=(jj == 0), stop=False),
                            reads=bufres + rcols(A2_GT + j * 512, A2_GT + (j + 1) * 512),
                            writes=([("pA", "pB", "pC")[t4]] if t4 < 3 else pDres(o, o + 512)))
            for t4 in range(4):
                for o in range(0, 1024, 512):
                    E("tensor", lambda e, _t=t4, o0=o: e.matmul(
                        p[_t][:, o0:o0 + 512], oneb[0:1, 0:128], br2[0:1, o0:o0 + 512],
                        start=False, stop=True),
                        reads=["oneb", "br2"],
                        writes=([("pA", "pB", "pC")[t4]] if t4 < 3 else pDres(o, o + 512)))

        def a2_x1(tt):
            for t4 in range(4):
                c = tt * 4 + t4
                pres = [("pA", "pB", "pC")[t4]] if t4 < 3 else ["pD_b0", "pD_b1"]
                E("gpsimd", lambda e, cc=c: e.dma_start(
                    out=qf[:], in_=d["xa_h"][cc * 128:(cc + 1) * 128, :]),
                    reads=["xa_h"], writes=["qf"], dma=True)
                E("gpsimd", lambda e, cc=c: e.dma_start(
                    out=kf[:], in_=d["x0"][cc * 128:(cc + 1) * 128, :]),
                    writes=["kf"], dma=True)
                E("vector", lambda e, _t=t4: e.scalar_tensor_tensor(
                    vf[:], p[_t][:], 1.0, qf[:], ALU.mult, ALU.add),
                    reads=pres + ["qf", "vf"], writes=["vf"])
                E("vector", lambda e: e.tensor_tensor(vf[:], vf[:], kf[:], ALU.add),
                  reads=["vf", "kf"], writes=["vf"])
                E("sync", lambda e, cc=c: e.dma_start(
                    out=d["x1_h"][cc * 128:(cc + 1) * 128, :], in_=vf[:]),
                    reads=["vf"], writes=["x1_h"], dma=True)
                m3, r3, m3r, r3r = stats_ln(vf[:], "vf", "3")
                E("vector", lambda e, cc=c: e.tensor_copy(ystat[:, cc:cc + 1], m3),
                  reads=[m3r], writes=["ystat_m"])
                E("vector", lambda e, cc=c: e.tensor_copy(ystat[:, 32 + cc:33 + cc], r3),
                  reads=[r3r], writes=["ystat_r"])
                E("vector", lambda e: e.tensor_scalar(
                    scr[:], vf[:], m3, r3, ALU.subtract, ALU.mult),
                    reads=["vf", m3r, r3r, "scr"], writes=["scr"])
                E("vector", lambda e: e.tensor_reduce(
                    smcol("ymx"), scr[:], AX, ALU.max, apply_absolute_value=True),
                    reads=["scr"], writes=["sm_ymx"])
                E("vector", lambda e: e.tensor_tensor(
                    ystat[:, 64:65], ystat[:, 64:65], smcol("ymx"), ALU.max),
                    reads=["sm_ymx", "ystat_ymx"], writes=["ystat_ymx"])

        a2_cs(0)
        for tt in range(NTT):
            a2_w1(tt)
            a2_w2(tt)
            if tt + 1 < NTT:
                a2_cs(tt + 1)
            a2_x1(tt)

        deq_drain()

        def allreduce_scalar(k, srccols, outcols):
            for i, (ap_, res_) in enumerate(srccols):
                E("tensor", lambda e, a=ap_, ii=i: e.transpose(
                    pD[0:1, 512 + ii * 128:512 + (ii + 1) * 128], a, idf[:]),
                    reads=[res_, "c_idf"], writes=["pD_b1"])
            for i in range(len(srccols)):
                E("vector", lambda e, ii=i: e.tensor_reduce(
                    smcol("ccv", 2)[0:1, ii:ii + 1],
                    pD[0:1, 512 + ii * 128:512 + (ii + 1) * 128], AX, ALU.max),
                    reads=["pD_b1"], writes=["sm_ccv"])
            E("gpsimd", lambda e, kk=k: e.dma_start(
                out=d[f"ccb{kk}"], in_=smcol("ccv", 2)[0:1, :]),
                reads=["sm_ccv"], writes=[f"ccb{k}"], dma=True)
            E("gpsimd", lambda e, kk=k: e.collective_compute(
                "AllReduce", ALU.max, replica_groups=[list(range(8))],
                ins=[d[f"ccb{kk}"].opt()], outs=[d[f"ccr{kk}"].opt()]),
                reads=[f"ccb{k}"], writes=[f"ccr{k}"])
            E("gpsimd", lambda e, kk=k: e.dma_start(
                out=smcol("ccv", 2)[0:1, :], in_=d[f"ccr{kk}"]),
                reads=[f"ccr{k}"], writes=["sm_ccv"], dma=True)
            for i, (ap_, res_) in enumerate(outcols):
                E("tensor", lambda e, ii=i: e.matmul(
                    pD[:, 512 + ii:513 + ii], onesf[:], smcol("ccv", 2)[0:1, ii:ii + 1],
                    start=True, stop=True),
                    reads=["sm_ccv", "onesf"], writes=["pD_b1"])
                E("vector", lambda e, a=ap_, ii=i: e.tensor_copy(
                    a, pD[:, 512 + ii:513 + ii]),
                    reads=["pD_b1"], writes=[res_])

        sch.barrier_all()
        allreduce_scalar(0, [(ystat[:, 64:65], "ystat_ymx")],
                         [(ystat[:, 65:66], "ystat_g")])
        E("vector", lambda e: e.tensor_scalar(
            ystat[:, 66:67], ystat[:, 65:66], 1.0 / 32767.0, 1e-12, ALU.mult, ALU.max),
            reads=["ystat_g"], writes=["ystat_sy"])
        E("vector", lambda e: e.reciprocal(ystat[:, 67:68], ystat[:, 66:67]),
          reads=["ystat_sy"], writes=["ystat_syi"])
        E("vector", lambda e: e.tensor_tensor(
            ystat[:, 32:32 + NCH], ystat[:, 32:32 + NCH],
            ystat[:, 67:68].broadcast_to([128, NCH]), ALU.mult),
            reads=["ystat_r", "ystat_syi"], writes=["ystat_r"])

        QTOK = min(1024, BW)
        NQ = BW // QTOK
        NCQ = QTOK // 128
        NMM2 = QTOK // 512
        YQ0 = [A2_HT, A2_HT + 8 * QTOK]

        def yprep_chunk(q, cs):
            c = q * NCQ + cs
            Y = YQ0[q % 2]
            xb, xbres = x0t[cs % 2], f"x0t{cs % 2}"
            E("gpsimd", lambda e, cc=c, b=xb: e.dma_start(
                out=b[:], in_=d["x1_h"][cc * 128:(cc + 1) * 128, :]),
                reads=["x1_h"], writes=[xbres], dma=True)
            E("vector", lambda e, cc=c, b=xb: e.tensor_scalar(
                scr[:], b[:], ystat[:, cc:cc + 1], ystat[:, 32 + cc:33 + cc],
                ALU.subtract, ALU.mult),
                reads=[xbres, "ystat_m", "ystat_r", "scr"], writes=["scr"])
            E("vector", lambda e: e.tensor_scalar(
                scr[:], scr[:], MAGIC, MAGIC, ALU.add, ALU.subtract),
                reads=["scr"], writes=["scr"])
            E("vector", lambda e: e.tensor_scalar(
                scr[:], scr[:], -32768.0, 32767.0, ALU.max, ALU.min),
                reads=["scr"], writes=["scr"])
            E("vector", lambda e: e.tensor_scalar(
                rap("nb"), scr[:], ystat[:, 66:67], None, ALU.mult),
                reads=["scr", "ystat_sy"], writes=rres("nb"))
            for j in range(8):
                E("sync", lambda e, jj=j, css=cs, Y_=Y: e.dma_start_transpose(
                    rb[:, Y_ + jj * QTOK + css * 128:Y_ + jj * QTOK + (css + 1) * 128],
                    rap("nb", jj * 128, (jj + 1) * 128)),
                    reads=rres("nb"),
                    writes=rcols(Y + j * QTOK + cs * 128,
                                 Y + j * QTOK + (cs + 1) * 128), dma=True)

        for cs0 in range(NCQ):
            yprep_chunk(0, cs0)
        WSLOT = 4
        upb = [wqb[:, sl * 1024:(sl + 1) * 1024] for sl in range(WSLOT)]
        gtb = [wkb[:, sl * 1024:(sl + 1) * 1024] for sl in range(WSLOT)]
        seq = [(q, oc, m) for q in range(NQ) for oc in range(24) for m in range(2)]
        PF = 4

        def emit_load(i):
            q_, oc_, m_ = seq[i]
            idx = q_ * 24 + oc_
            sl = idx % WSLOT
            b = (upb, gtb)[m_][sl]
            w = ("upT_h", "gtT_h")[m_]
            res = [f"{('ups', 'gts')[m_]}{sl}"]
            E("gpsimd", lambda e, occ=oc_, b=b, w=w: e.dma_start(
                out=b, in_=d[w][occ]),
                reads=[w], writes=res, dma=True)
            return b, res

        ldinfo = {}
        for i in range(min(PF, len(seq))):
            ldinfo[i] = emit_load(i)
        for i, (q, oc, m) in enumerate(seq):
            if i + PF < len(seq):
                ldinfo[i + PF] = emit_load(i + PF)
            wb, wres = ldinfo.pop(i)
            Y = YQ0[q % 2]
            yres = rcols(Y, Y + 8 * QTOK)
            idx = q * 24 + oc
            if m == 0:
                pp = pA if idx % 2 == 0 else pC
                pres_of = (lambda nn, nm=("pA" if idx % 2 == 0 else "pC"): [nm])
                sti, hb = 0, "rup_h"
            else:
                if idx % 2 == 0:
                    pp = pB
                    pres_of = (lambda nn: ["pB"])
                else:
                    pp = pD
                    pres_of = (lambda nn: pDres(nn * 512, (nn + 1) * 512))
                sti, hb = 1, "rgt_h"
            for j in range(8):
                for n in range(NMM2):
                    E("tensor", lambda e, jj=j, nn=n, b=wb, _p=pp, Y_=Y: e.matmul(
                        _p[:, nn * 512:(nn + 1) * 512],
                        b[:, jj * 128:(jj + 1) * 128],
                        rb[:, Y_ + jj * QTOK + nn * 512:Y_ + jj * QTOK + (nn + 1) * 512],
                        start=(jj == 0), stop=(jj == 7)),
                        reads=wres + rcols(Y + j * QTOK + n * 512,
                                           Y + j * QTOK + (n + 1) * 512),
                        writes=pres_of(n))
            so = (oc % 2) * NMM2
            sres = [f"stg{sti}_{so + n}" for n in range(NMM2)]
            for n in range(NMM2):
                E("scalar", lambda e, _p=pp, s=sti, nn=n, so_=so: e.copy(
                    stg[s][:, (so_ + nn) * 512:(so_ + nn + 1) * 512],
                    _p[:, nn * 512:(nn + 1) * 512]),
                    reads=pres_of(n), writes=[sres[n]])
            E("gpsimd", lambda e, occ=oc, s=sti, so_=so, h_=hb, qq=q: e.dma_start(
                out=d[h_][occ, :, qq * QTOK:(qq + 1) * QTOK],
                in_=stg[s][:, so_ * 512:(so_ + NMM2) * 512]),
                reads=sres, writes=[hb], dma=True)
            mcol = smcol("ymx2", 48)[:, sti * 24 + oc:sti * 24 + oc + 1]
            mres = f"sm_ymx2_{sti}_{oc}"
            if q == 0:
                E("vector", lambda e, s=sti, so_=so, mc=mcol: e.tensor_reduce(
                    mc, stg[s][:, so_ * 512:(so_ + NMM2) * 512], AX,
                    ALU.max, apply_absolute_value=True),
                    reads=sres, writes=[mres])
            else:
                tcol = smcol("ymx3", 48)[:, sti * 24 + oc:sti * 24 + oc + 1]
                tres = f"sm_ymx3_{sti}_{oc}"
                E("vector", lambda e, s=sti, so_=so, tc=tcol: e.tensor_reduce(
                    tc, stg[s][:, so_ * 512:(so_ + NMM2) * 512], AX,
                    ALU.max, apply_absolute_value=True),
                    reads=sres, writes=[tres])
                E("vector", lambda e, mc=mcol, tc=tcol: e.tensor_tensor(
                    mc, mc, tc, ALU.max),
                    reads=[tres, mres], writes=[mres])
            if m == 1 and q + 1 < NQ and oc < NCQ:
                yprep_chunk(q + 1, oc)

        E("vector", lambda e: e.tensor_reduce(
            ystat[:, 68:69], smcol("ymx2", 48)[:, 0:24], AX, ALU.max),
            reads=[f"sm_ymx2_0_{o}" for o in range(24)], writes=["ystat_um"])
        E("vector", lambda e: e.tensor_reduce(
            ystat[:, 69:70], smcol("ymx2", 48)[:, 24:48], AX, ALU.max),
            reads=[f"sm_ymx2_1_{o}" for o in range(24)], writes=["ystat_gm"])
        allreduce_scalar(1, [(ystat[:, 68:69], "ystat_um"), (ystat[:, 69:70], "ystat_gm")],
                         [(ystat[:, 70:71], "ystat_umg"), (ystat[:, 71:72], "ystat_gmg")])
        for (srcc, sres, oc_, ores) in ((70, "ystat_umg", 72, "ystat_su"),
                                        (71, "ystat_gmg", 74, "ystat_sg")):
            E("vector", lambda e, s=srcc, o=oc_: e.tensor_scalar(
                ystat[:, o:o + 1], ystat[:, s:s + 1], 1.0 / 32767.0, 1e-12,
                ALU.mult, ALU.max),
                reads=[sres], writes=[ores])
            E("vector", lambda e, o=oc_: e.reciprocal(ystat[:, o + 1:o + 2],
                                                      ystat[:, o:o + 1]),
              reads=[ores], writes=[ores + "i"])

        E("vector", lambda e: e.memset(ystat[:, 76:77], 0.0), writes=["ystat_tm"])
        ALLS = [f"stg0_{k}" for k in range(8)]
        ALLS1 = [f"stg1_{k}" for k in range(8)]
        sg2p = [rb[:, 8192:16384].bitcast(F32), rb[:, 16384:24576].bitcast(F32)]
        ALLS2p = [rcols(8192, 16384), rcols(16384, 24576)]
        def b2_ops(oc):
            ops = []

            def E2(eng, fn, **kw):
                ops.append(lambda e_=eng, f_=fn, k_=kw: E(e_, f_, **k_))
            bt = stg[oc % 2]
            sg2 = sg2p[oc % 2]
            ALLS2 = ALLS2p[oc % 2]
            btres = ALLS if oc % 2 == 0 else ALLS1
            E2("gpsimd", lambda e, occ=oc, b=bt: e.dma_start(
                out=b[:, 0:BW], in_=d["rgt_h"][occ]),
                reads=["rgt_h"], writes=btres, dma=True)
            E2("scalar", lambda e, b=bt: e.activation(
                b[:, 0:BW], b[:, 0:BW], ACTF.Identity, bias=MAGIC, scale=ystat[:, 75:76]),
                reads=btres + ["ystat_sgi", "cst"], writes=btres)
            E2("scalar", lambda e, b=bt: e.activation(
                b[:, 0:BW], b[:, 0:BW], ACTF.Identity, bias=-MAGIC),
                reads=btres + ["cst"], writes=btres)
            E2("vector", lambda e, b=bt: e.tensor_scalar(
                b[:, 0:BW], b[:, 0:BW], -32768.0, 32767.0, ALU.max, ALU.min),
                reads=btres, writes=btres)
            E2("scalar", lambda e, b=bt: e.activation(
                b[:, 0:BW], b[:, 0:BW], ACTF.Identity, scale=ystat[:, 74:75]),
                reads=btres + ["ystat_sg"], writes=btres)
            E2("scalar", lambda e, b=bt, s2=sg2: e.activation(s2[:, 0:BW], b[:, 0:BW], ACTF.Sigmoid),
              reads=btres + ALLS2, writes=ALLS2)
            E2("scalar", lambda e, s2=sg2: e.activation(
                s2[:, 0:BW], s2[:, 0:BW], ACTF.Identity, bias=MAGIC, scale=cst[:, 3:4]),
                reads=ALLS2 + ["cst"], writes=ALLS2)
            E2("vector", lambda e, s2=sg2: e.tensor_scalar(
                s2[:, 0:BW], s2[:, 0:BW], -MAGIC, 0.0, ALU.add, ALU.max),
                reads=ALLS2, writes=ALLS2)
            E2("vector", lambda e, s2=sg2: e.tensor_scalar(
                s2[:, 0:BW], s2[:, 0:BW], 65535.0, 1.0 / 65536.0, ALU.min, ALU.mult),
                reads=ALLS2, writes=ALLS2)
            E2("vector", lambda e, b=bt, s2=sg2: e.tensor_tensor(b[:, 0:BW], b[:, 0:BW], s2[:, 0:BW], ALU.mult),
              reads=btres + ALLS2, writes=btres)
            E2("vector", lambda e, b=bt: e.tensor_reduce(
                smcol("um2"), b[:, 0:BW], AX, ALU.max, apply_absolute_value=True),
                reads=btres, writes=["sm_um2"])
            E2("vector", lambda e: e.tensor_tensor(
                ystat[:, 76:77], ystat[:, 76:77], smcol("um2"), ALU.max),
                reads=["sm_um2", "ystat_tm"], writes=["ystat_tm"])
            E2("gpsimd", lambda e, occ=oc, b=bt: e.dma_start(
                out=d["tt_h"][occ], in_=b[:, 0:BW]),
                reads=btres, writes=["tt_h"], dma=True)

            return ops

        import itertools as _it
        for k2 in range(12):
            for a_, b_ in _it.zip_longest(b2_ops(2 * k2), b2_ops(2 * k2 + 1)):
                if a_ is not None:
                    a_()
                if b_ is not None:
                    b_()

        allreduce_scalar(2, [(ystat[:, 76:77], "ystat_tm")],
                         [(ystat[:, 77:78], "ystat_tmg")])
        E("vector", lambda e: e.tensor_scalar(
            ystat[:, 78:79], ystat[:, 77:78], 1.0 / 32767.0, 1e-12, ALU.mult, ALU.max),
            reads=["ystat_tmg"], writes=["ystat_st"])
        E("vector", lambda e: e.reciprocal(ystat[:, 79:80], ystat[:, 78:79]),
          reads=["ystat_st"], writes=["ystat_sti"])

        E("vector", lambda e: e.memset(smcol("smx"), 0.0), writes=["sm_smx"])
        def b3_ops(oc):
            ops = []

            def E2(eng, fn, **kw):
                ops.append(lambda e_=eng, f_=fn, k_=kw: E(e_, f_, **k_))
            bt3 = stg[oc % 2] if True else None
            bu3 = sg2p[oc % 2]
            R3t = ALLS if oc % 2 == 0 else ALLS1
            R3u = ALLS2p[oc % 2]
            E2("gpsimd", lambda e, occ=oc, b=bt3: e.dma_start(out=b[:, 0:BW], in_=d["tt_h"][occ]),
              reads=["tt_h"], writes=R3t, dma=True)
            E2("gpsimd", lambda e, occ=oc, b=bu3: e.dma_start(out=b[:, 0:BW], in_=d["rup_h"][occ]),
              reads=["rup_h"], writes=R3u, dma=True)
            E2("scalar", lambda e: e.activation(
                bt3[:, 0:BW], bt3[:, 0:BW], ACTF.Identity, bias=MAGIC, scale=ystat[:, 79:80]),
                reads=R3t + ["ystat_sti", "cst"], writes=R3t)
            E2("scalar", lambda e: e.activation(
                bt3[:, 0:BW], bt3[:, 0:BW], ACTF.Identity, bias=-MAGIC),
                reads=R3t + ["cst"], writes=R3t)
            E2("vector", lambda e: e.tensor_scalar(
                bt3[:, 0:BW], bt3[:, 0:BW], -32768.0, 32767.0, ALU.max, ALU.min),
                reads=R3t, writes=R3t)
            E2("vector", lambda e: e.tensor_scalar(
                bu3[:, 0:BW], bu3[:, 0:BW], ystat[:, 73:74], MAGIC, ALU.mult, ALU.add),
                reads=R3u + ["ystat_sui"], writes=R3u)
            E2("vector", lambda e: e.tensor_scalar(
                bu3[:, 0:BW], bu3[:, 0:BW], MAGIC - 32768.0, MAGIC + 32767.0,
                ALU.max, ALU.min),
                reads=R3u, writes=R3u)
            E2("scalar", lambda e: e.activation(
                bu3[:, 0:BW], bu3[:, 0:BW], ACTF.Identity, bias=-MAGIC),
                reads=R3u + ["cst"], writes=R3u)
            E2("vector", lambda e, b=bt3, b2_=bu3: e.tensor_tensor(b[:, 0:BW], b[:, 0:BW], b2_[:, 0:BW], ALU.mult),
              reads=R3t + R3u, writes=R3t)
            E2("vector", lambda e, b=bt3: e.tensor_reduce(
                smcol("um3", 1), b[:, 0:BW], AX, ALU.max, apply_absolute_value=True),
                reads=R3t, writes=["sm_um3"])
            E2("vector", lambda e: e.tensor_tensor(
                smcol("smx"), smcol("smx"), smcol("um3", 1), ALU.max),
                reads=["sm_um3", "sm_smx"], writes=["sm_smx"])
            E2("gpsimd", lambda e, occ=oc, b=bt3: e.dma_start(out=d["ss_h"][occ], in_=b[:, 0:BW]),
              reads=R3t, writes=["ss_h"], dma=True)

            return ops

        for k3 in range(12):
            for a_, b_ in _it.zip_longest(b3_ops(2 * k3), b3_ops(2 * k3 + 1)):
                if a_ is not None:
                    a_()
                if b_ is not None:
                    b_()

        allreduce_scalar(3, [(smcol("smx"), "sm_smx")], [(smcol("smg"), "sm_smg")])
        E("vector", lambda e: e.tensor_scalar(
            smcol("ssc"), smcol("smg"), 1.0 / 32767.0, 1e-12, ALU.mult, ALU.max),
            reads=["sm_smg"], writes=["sm_ssc"])
        E("vector", lambda e: e.reciprocal(smcol("ssi"), smcol("ssc")),
          reads=["sm_ssc"], writes=["sm_ssi"])
        E("vector", lambda e: e.tensor_tensor(
            ystat[:, 80:81], ystat[:, 78:79], ystat[:, 72:73], ALU.mult),
            reads=["ystat_st", "ystat_su"], writes=["ystat_stsu"])
        E("vector", lambda e: e.tensor_tensor(
            smcol("sso"), smcol("ssc"), ystat[:, 80:81], ALU.mult),
            reads=["sm_ssc", "ystat_stsu"], writes=["sm_sso"])

        for tt in range(NTT):
            for oc in range(24):
                sl4 = (oc % 8) // 2
                bt = stg[oc % 2][:, sl4 * 512:(sl4 + 1) * 512]
                btres = [f"stg{oc % 2}_{sl4}"]
                E("gpsimd", lambda e, occ=oc, ttt=tt, b=bt: e.dma_start(
                    out=b, in_=d["ss_h"][occ, :, ttt * 512:(ttt + 1) * 512]),
                    reads=["ss_h"], writes=btres, dma=True)
                E("vector", lambda e, b=bt: e.tensor_scalar(
                    b, b, smcol("ssi"), MAGIC, ALU.mult, ALU.add),
                    reads=btres + ["sm_ssi"], writes=btres)
                E("vector", lambda e, b=bt: e.tensor_scalar(
                    b, b, MAGIC - 32768.0, MAGIC + 32767.0, ALU.max, ALU.min),
                    reads=btres, writes=btres)
                E("scalar", lambda e, occ=oc, b=bt: e.activation(
                    rb[:, A2_GT + occ * 512:A2_GT + (occ + 1) * 512], b,
                    ACTF.Identity, bias=-MAGIC),
                    reads=btres + ["cst"],
                    writes=rcols(A2_GT + oc * 512, A2_GT + (oc + 1) * 512))
            for j in range(24):
                gi = tt * 24 + j
                if tt == 0 and j < 4:
                    sl0 = gi % 8
                    E("gpsimd", lambda e, jj=j, b=wvb[:, sl0 * 1024:(sl0 + 1) * 1024]:
                      e.dma_start(out=b, in_=d["dnT_h"][jj]),
                        reads=["dnT_h"], writes=[f"dns{sl0}"], dma=True)
                if gi + 4 < NTT * 24:
                    jn, sn = (gi + 4) % 24, (gi + 4) % 8
                    E("gpsimd", lambda e, jj=jn, b=wvb[:, sn * 1024:(sn + 1) * 1024]:
                      e.dma_start(out=b, in_=d["dnT_h"][jj]),
                        reads=["dnT_h"], writes=[f"dns{sn}"], dma=True)
                sl = gi % 8
                buf = wvb[:, sl * 1024:(sl + 1) * 1024]
                jres = rcols(A2_GT + j * 512, A2_GT + (j + 1) * 512)
                for t4 in range(4):
                    for o in range(0, 1024, 512):
                        E("tensor", lambda e, jj=j, _t=t4, o0=o, b=buf: e.matmul(
                            p[_t][:, o0:o0 + 512],
                            rb[:, A2_GT + jj * 512 + _t * 128:A2_GT + jj * 512 + (_t + 1) * 128],
                            b[:, o0:o0 + 512], start=(jj == 0), stop=(jj == 23)),
                            reads=[f"dns{sl}"] + jres,
                            writes=([("pA", "pB", "pC")[t4]] if t4 < 3 else pDres(o, o + 512)))
            for t4 in range(4):
                c = tt * 4 + t4
                pres = [("pA", "pB", "pC")[t4]] if t4 < 3 else ["pD_b0", "pD_b1"]
                xb, xbres = x0t[t4 % 2], f"x0t{t4 % 2}"
                E("gpsimd", lambda e, cc=c, b=xb: e.dma_start(
                    out=b[:], in_=d["x1_h"][cc * 128:(cc + 1) * 128, :]),
                    reads=["x1_h"], writes=[xbres], dma=True)
                E("vector", lambda e, _t=t4, b=xb: e.scalar_tensor_tensor(
                    scr[:], p[_t][:], smcol("sso"), b[:], ALU.mult, ALU.add),
                    reads=pres + [xbres, "scr", "sm_sso"], writes=["scr"])
                E("sync", lambda e, cc=c: e.dma_start(
                    out=d["out"][cc * 128:(cc + 1) * 128, :], in_=scr[:]),
                    reads=["scr"], writes=["out_h"], dma=True)

        for nm2, sem2 in sch.dmapool["sync"]:
            v2 = sch.count[nm2]
            if v2 > 0:
                sch.stream["sync"].append(lambda e, s=sem2, v=v2: e.wait_ge(s, v))

        sch.finalize(block)
    return nc


def kernel(**inputs):
    cons = _consts_np()
    common = {}
    for k in ("Wq", "Wk", "Wv", "Wa", "Wb", "Wo", "W1", "b1", "W2", "b2"):
        common[k] = np.ascontiguousarray(np.asarray(inputs["L0_attn_" + k], np.float32))
    common["ln1w"] = np.ascontiguousarray(np.asarray(inputs["L0_attn_ln1_w"], np.float32))
    common["ln1b"] = np.ascontiguousarray(np.asarray(inputs["L0_attn_ln1_b"], np.float32))
    common["ln2w"] = np.ascontiguousarray(np.asarray(inputs["L0_attn_ln2_w"], np.float32))
    common["ln2b"] = np.ascontiguousarray(np.asarray(inputs["L0_attn_ln2_b"], np.float32))
    common["wup"] = np.ascontiguousarray(np.asarray(inputs["L0_ffn_w_up"], np.float32))
    common["wgt"] = np.ascontiguousarray(np.asarray(inputs["L0_ffn_w_gate"], np.float32))
    common["wdn"] = np.ascontiguousarray(np.asarray(inputs["L0_ffn_w_down"], np.float32))
    common.update(cons)
    x = np.asarray(inputs["L0_attn_x"], np.float32)
    in_maps = []
    for c in range(8):
        m = dict(common)
        m["x0"] = np.ascontiguousarray(x[c])
        in_maps.append(m)
    nc = build_program()
    res = run_bass_kernel_spmd(nc, in_maps, core_ids=list(range(8)))
    return np.stack([res.results[c]["out"] for c in range(8)], axis=0)
